# version 09
# baseline (speedup 1.0000x reference)
import math
import numpy as np
import ml_dtypes
from contextlib import ExitStack

import concourse.bass as bass
import concourse.mybir as mybir
from concourse.bass_utils import run_bass_kernel_spmd

F32 = mybir.dt.float32
BF16 = mybir.dt.bfloat16
I32 = mybir.dt.int32
AF = mybir.ActivationFunctionType
ALU = mybir.AluOpType
AX = mybir.AxisListType

POOL_W = 48128
NEGBIG = -1.0e30
AOFF = 65536.0
AOFF_SC = AOFF * 1e-4

QUEUES = ("SYNC", "GPQ", "PE", "ACT", "DVE")
CLASSES = {
    "POOL": ("GPQ", 1),
    "PE": ("PE", 1),
    "ACT": ("ACT", 1),
    "DVE": ("DVE", 1),
}


class Buf:
    __slots__ = ("name", "w", "readers", "slices")

    def __init__(self, name, slices=False):
        self.name = name
        self.w = None
        self.readers = []
        self.slices = slices


class Step:
    __slots__ = ("cls", "fn", "waits", "idx", "force_inc", "slot", "seq")

    def __init__(self, cls, fn, waits, idx, force_inc, slot=None):
        self.cls = cls
        self.fn = fn
        self.waits = waits
        self.idx = idx
        self.force_inc = force_inc
        self.slot = slot
        self.seq = -1


class Slot:
    __slots__ = ("name", "count")

    def __init__(self, name):
        self.name = name
        self.count = 0


class Prog:

    def __init__(self, pool_on_dve=False):
        self.pool_on_dve = pool_on_dve
        self.seq = 0
        self.steps = {q: [] for q in QUEUES}
        self.icount = {c: 0 for c in CLASSES}
        self.waited = {q: {} for q in QUEUES}
        self.used = set()
        self.slots = {}

    def slot(self, name):
        if name not in self.slots:
            self.slots[name] = Slot(name)
        return self.slots[name]

    def _resolve(self, q, want, marker):
        waits = []
        wmap = self.waited[q]
        for m in want:
            if m == marker:
                continue
            if m[0] == "S":
                val = self.slots[m[1]].count
                m = ("S", m[1], val)
                key = ("S", m[1])
            else:
                key, val = m[0], m[1]
            if wmap.get(key, -1) >= val:
                continue
            wmap[key] = val
            if m[0] != "S":
                self.used.add(m)
            waits.append(m)
        return waits

    def _track(self, reads, writes, marker):
        want = []
        for b in reads:
            if b.w is not None:
                want.append(b.w)
        for b in writes:
            if b.w is not None and not b.slices:
                want.append(b.w)
            want.extend(b.readers)
        return want

    def step(self, cls, fn, reads=(), writes=(), force_inc=False):
        if cls == "POOL" and self.pool_on_dve:
            cls = "DVE"
        q = CLASSES[cls][0]
        idx = self.icount[cls]
        self.icount[cls] += 1
        marker = (cls, idx)
        want = self._track(reads, writes, marker)
        waits = self._resolve(q, want, marker)
        st = Step(cls, fn, waits, idx, force_inc)
        st.seq = self.seq
        self.seq += 1
        self.steps[q].append(st)
        for b in writes:
            b.w = marker
            b.readers = []
        for b in reads:
            b.readers.append(marker)
        return marker

    def dma(self, q, slot_name, fn, reads=(), writes=()):
        sl = self.slot(slot_name)
        want = self._track(reads, writes, None)
        waits = self._resolve(q, want, None)
        sl.count += 16
        marker = ("S", sl.name, sl.count)
        st = Step(None, fn, waits, None, False, slot=sl.name)
        st.seq = self.seq
        self.seq += 1
        self.steps[q].append(st)
        for b in writes:
            b.w = marker
            b.readers = []
        for b in reads:
            b.readers.append(marker)
        return marker

    def wait_only(self, q, markers):
        waits = self._resolve(q, markers, None)
        if waits:
            st = Step(None, None, waits, None, False)
            st.seq = self.seq
            self.seq += 1
            self.steps[q].append(st)

    def barrier(self):
        last = [(c, self.icount[c] - 1) for c in CLASSES if self.icount[c] > 0]
        last += [("S", sl.name, sl.count) for sl in self.slots.values() if sl.count > 0]
        for q in QUEUES:
            self.wait_only(q, last)

    def finalize(self):
        for q in QUEUES:
            for st in self.steps[q]:
                if st.force_inc and st.cls is not None:
                    self.used.add((st.cls, st.idx))
        self.rank = {}
        for c in CLASSES:
            idxs = sorted(i for (cc, i) in self.used if cc == c)
            for r, i in enumerate(idxs):
                self.rank[(c, i)] = r + 1

    def emit_queue(self, q, eng, sems, ssems):
        for st in self.steps[q]:
            for m in st.waits:
                if m[0] == "S":
                    eng.wait_ge(ssems[m[1]], m[2])
                else:
                    eng.wait_ge(sems[m[0]], self.rank[m])
            if st.fn is not None:
                inst = st.fn(eng)
                if st.slot is not None:
                    inst.then_inc(ssems[st.slot], 16)
                elif (st.cls, st.idx) in self.used:
                    inst.then_inc(sems[st.cls], CLASSES[st.cls][1])


class Carver:
    def __init__(self, width):
        self.width = width
        self.off = 0
        self.marks = []
        self.peak = 0

    def take(self, n_f32, align=16):
        o = (self.off + align - 1) // align * align
        assert o + n_f32 <= self.width, f"SBUF pool overflow: {o}+{n_f32} > {self.width}"
        self.off = o + n_f32
        self.peak = max(self.peak, self.off)
        return o

    def push(self):
        self.marks.append(self.off)

    def pop(self):
        self.off = self.marks.pop()


def _splits(width, maxw=512):
    o = 0
    out = []
    while o < width:
        w = min(maxw, width - o)
        out.append((o, w))
        o += w
    return out


def build_nc(S, E, trunc=None, pool_on_dve=False):
    NCH = E // 128
    NSTRIP = S // 512
    NB = S // 512
    OWNT = S // 256
    NSO = OWNT // 4
    RSQ = 1.0 / math.sqrt(E)
    invE = 1.0 / E
    EH = E // 2
    PVW = max(EH, 512)

    nc = bass.Bass("TRN2", target_bir_lowering=False)

    emb = nc.dram_tensor("emb", [S, E], F32, kind="ExternalInput").ap()
    x_own = nc.dram_tensor("x_own", [OWNT * 128, E], F32, kind="ExternalInput").ap()
    wkT = nc.dram_tensor("wkT", [E, E], F32, kind="ExternalInput").ap()
    wqT = nc.dram_tensor("wqT", [E, E], F32, kind="ExternalInput").ap()
    wvT = nc.dram_tensor("wvT", [E, E], F32, kind="ExternalInput").ap()
    fker = nc.dram_tensor("fker", [E, E], F32, kind="ExternalInput").ap()
    fbias = nc.dram_tensor("fbias", [1, E], F32, kind="ExternalInput").ap()
    masksI = nc.dram_tensor("masks", [2, 128, 512], F32, kind="ExternalInput").ap()
    identI = nc.dram_tensor("ident", [128, 128], F32, kind="ExternalInput").ap()
    out_d = nc.dram_tensor("out", [OWNT * 128, E], F32, kind="ExternalOutput").ap()

    ktS = nc.dram_tensor("ktS", [NCH, 128, S], BF16, kind="Internal").ap()
    vS = nc.dram_tensor("vS", [S, E], BF16, kind="Internal").ap()
    qtS = nc.dram_tensor("qtS", [NCH, OWNT // 4, 128, 512], BF16, kind="Internal").ap()
    nxS = nc.dram_tensor("nxS", [OWNT, NCH, 128, 128], BF16, kind="Internal").ap()
    nyS = nc.dram_tensor("nyS", [OWNT * 128, E], BF16, kind="Internal").ap()
    rowS = nc.dram_tensor("rowS", [1, E], F32, kind="Internal").ap()

    p = Prog(pool_on_dve=pool_on_dve)
    es = ExitStack()
    with es:
        pool_h = es.enter_context(nc.sbuf_tensor("pool", [128, POOL_W], F32))
        ps_h = es.enter_context(nc.psum_tensor("pspool", [128, 4096], F32))
        sems = {}
        for c, nm in [("POOL", "sPL"), ("PE", "sPE"), ("ACT", "sAC"), ("DVE", "sDV")]:
            sems[c] = es.enter_context(nc.semaphore(nm))
        block = es.enter_context(nc.Block())

        pool_f = pool_h.ap()
        pool_b = pool_h.bitcast(BF16).ap()
        pool_i = pool_h.bitcast(I32).ap()
        psf = ps_h.ap()
        psb = ps_h.bitcast(BF16).ap()

        def cf(off, w):
            return pool_f[:, off:off + w]

        def cb(off, w):
            return pool_b[:, 2 * off:2 * off + w]

        def cI(off, w):
            return pool_i[:, off:off + w]

        al = Carver(POOL_W)

        psA = psf[:, 0:1024]
        psB = psf[:, 1024:3072]
        psTh = [psb[:, 6144:6656], psb[:, 7168:7680]]
        psN = psf[:, 3072:4096]
        psF = psf[:, 3328:3584]
        bA = [Buf("psA0"), Buf("psA1")]
        bB = [Buf("psB0"), Buf("psB1")]
        bT = [Buf("psT0"), Buf("psT1")]
        bF = [Buf("psF0"), Buf("psF1")]

        o_identf = al.take(128)
        o_identb = al.take(64)
        o_ones = al.take(128)
        o_masks = al.take(1024)
        o_srow = al.take(max(OWNT, 16))
        o_sy = al.take(max(OWNT, 16))
        o_tiny = al.take(64)
        identf = cf(o_identf, 128)
        identb = cb(o_identb, 128)
        ones_f = cf(o_ones, 128)
        mask_sb = cf(o_masks, 1024)
        srowP = cf(o_srow, OWNT)
        syP = cf(o_sy, OWNT)

        def tiny(g, i):
            return cf(o_tiny + 8 * g + i, 1)

        b_identf = Buf("identf")
        b_identb = Buf("identb")
        b_ones = Buf("ones")
        b_masks = Buf("masks")
        b_srow = Buf("srowP")
        b_sy = Buf("syP")
        b_tiny = [Buf(f"tiny{i}") for i in range(8)]
        b_ktS = Buf("ktS", slices=True)
        b_vS = Buf("vS", slices=True)
        b_qtS = Buf("qtS", slices=True)
        b_nxS = Buf("nxS", slices=True)
        b_nyS = Buf("nyS", slices=True)
        b_rowS = Buf("rowS")
        b_out = Buf("out", slices=True)

        p.dma("SYNC", "miscS", lambda e: e.dma_start(out=identf, in_=identI), writes=[b_identf])
        p.dma("GPQ", "identG", lambda e: e.dma_start(out=identb, in_=identI), writes=[b_identb])
        p.dma("SYNC", "miscS", lambda e: e.dma_start(out=mask_sb[:, 0:512], in_=masksI[0]), writes=[b_masks])
        p.dma("SYNC", "miscS", lambda e: e.dma_start(out=mask_sb[:, 512:1024], in_=masksI[1]), writes=[b_masks])
        p.step("DVE", lambda e: e.memset(ones_f[0:1, :], 1.0), writes=[b_ones])

        o_consts = al.take(8)
        b_const = Buf("consts")
        for i, v in enumerate([1e-5, RSQ, 128.0, AOFF, -128.0]):
            ap = cf(o_consts + i, 1)
            p.step("DVE", lambda e, ap=ap, v=v: e.memset(ap, v), writes=[b_const])
            nc.const_aps.register(F32, v, ap)
        p.barrier()

        def ln_tile(xap, b_x, xnb_ap, b_xnb, sq_ap, b_sq, tg):
            btg = b_tiny[tg]
            s1, s2, mm, var, sd, rr = (tiny(tg, i) for i in range(6))
            p.step("ACT", lambda e: e.activation(sq_ap, xap, AF.Copy, accum_out=s1),
                   reads=[b_x], writes=[b_sq, btg])
            p.step("ACT", lambda e: e.activation(sq_ap, xap, AF.Square, accum_out=s2),
                   reads=[b_x], writes=[b_sq, btg])
            p.step("DVE", lambda e: e.tensor_scalar(mm, s1, invE, None, op0=ALU.mult),
                   reads=[btg], writes=[btg])
            p.step("DVE", lambda e: e.tensor_scalar(var, s2, invE, None, op0=ALU.mult),
                   reads=[btg], writes=[btg])
            p.step("DVE", lambda e: e.scalar_tensor_tensor(sd, mm, 1.0, mm, op0=ALU.mult, op1=ALU.mult),
                   reads=[btg], writes=[btg])
            p.step("DVE", lambda e: e.tensor_sub(var, var, sd), reads=[btg], writes=[btg])
            p.step("ACT", lambda e: e.activation(sd, var, AF.Sqrt, bias=1e-5),
                   reads=[btg], writes=[btg])
            p.step("DVE", lambda e: e.reciprocal(rr, sd), reads=[btg], writes=[btg])
            negmr = tiny(tg, 4)
            p.step("DVE", lambda e: e.tensor_scalar(negmr, mm, rr, -1.0, op0=ALU.mult, op1=ALU.mult),
                   reads=[btg], writes=[btg])
            p.step("DVE", lambda e: e.tensor_scalar(xnb_ap, xap, rr, negmr, op0=ALU.mult, op1=ALU.add),
                   reads=[b_x, btg], writes=[b_xnb])
            return rr, negmr, mm

        def transpose_chunks(src_ap, b_src, nchunks, dst_fn, evac_cls="DVE", par=0):
            assert nchunks % 4 == 0
            for j in range(nchunks // 4):
                half = (j + par) % 2
                pst = psTh[half]
                for cc in range(4):
                    c = 4 * j + cc
                    psl = pst[:, 128 * cc:128 * cc + 128]
                    sap = src_ap[:, 128 * c:128 * c + 128]
                    p.step("PE", lambda e, o=psl, i=sap: e.transpose(o, i, identb),
                           reads=[b_src, b_identb], writes=[bT[half]])
                pv = pst[:, 0:512]
                dap, dbuf = dst_fn(j)
                if evac_cls == "ACT":
                    p.step("ACT", lambda e, o=dap, i=pv: e.activation(o, i, AF.Copy),
                           reads=[bT[half]], writes=[dbuf])
                else:
                    p.step(evac_cls, lambda e, o=dap, i=pv: e.tensor_copy(o, i),
                           reads=[bT[half]], writes=[dbuf])

        al.push()
        o_w1 = al.take(NCH * E // 2)
        o_w2 = al.take(NCH * E // 2)
        o_xd = [al.take(E), al.take(E)]
        o_xnb4 = al.take(4 * E // 2)
        o_xnT = al.take(NCH * 512 // 2)
        o_kev = al.take(512)
        o_vev = al.take(E // 2)
        W1 = cb(o_w1, NCH * E)
        W2 = cb(o_w2, NCH * E)
        xd = [cf(o_xd[0], E), cf(o_xd[1], E)]
        xnb4 = cb(o_xnb4, 4 * E)
        xnT = cb(o_xnT, NCH * 512)
        xnT3 = xnT.rearrange("p (c k) -> p c k", k=512)
        kev = [cb(o_kev, 512), cb(o_kev + 256, 512)]
        vev = cb(o_vev, E)
        b_W1 = Buf("W1", slices=True)
        b_W2 = Buf("W2", slices=True)
        b_xd = [Buf("xd0"), Buf("xd1")]
        b_xnb4 = [Buf(f"xnb4_{i}") for i in range(4)]
        b_xnT1 = Buf("xnT")
        b_xnT = [b_xnT1] * (NCH // 4)
        b_kev = [Buf("kev0"), Buf("kev1")]
        b_vev = Buf("vev")

        for c in range(NCH):
            p.dma("GPQ", "wload", lambda e, c=c: e.dma_start(
                out=W1[:, E * c:E * (c + 1)], in_=wkT[128 * c:128 * (c + 1), :]),
                writes=[b_W1])
            p.dma("GPQ", "wload", lambda e, c=c: e.dma_start(
                out=W2[:, E * c:E * (c + 1)], in_=wvT[128 * c:128 * (c + 1), :]),
                writes=[b_W2])

        def p1_ln(strip):
            for t in range(4):
                til = 4 * strip + t
                sl = til % 2
                xn_t = xnb4[:, E * t:E * (t + 1)]
                p.dma("SYNC", f"x{sl}", lambda e, til=til, sl=sl: e.dma_start(
                    out=xd[sl], in_=emb[128 * til:128 * (til + 1), :]), writes=[b_xd[sl]])
                ln_tile(xd[sl], b_xd[sl], xn_t, b_xnb4[t], xn_t, b_xnb4[t], t)

        def p1_transposes(strip):
            for t in range(4):
                transpose_chunks(
                    xnb4[:, E * t:E * (t + 1)], b_xnb4[t], NCH,
                    lambda j, t=t: (xnT3[:, 4 * j:4 * j + 4, 128 * t:128 * t + 128],
                                    b_xnT[j]), par=t)

        p1_ln(0)
        p1_transposes(0)
        for s in range(NSTRIP):
            for a in range(NCH):
                pa = psA[:, 512 * (a % 2):512 * (a % 2) + 512]
                for c in range(NCH):
                    p.step("PE", lambda e, pa=pa, a=a, c=c, st=(c == 0), sp=(c == NCH - 1):
                           e.matmul(pa, W1[:, E * c + 128 * a:E * c + 128 * a + 128],
                                    xnT[:, 512 * c:512 * c + 512], start=st, stop=sp),
                           reads=[b_W1, b_xnT[c // 4]], writes=[bA[a % 2]])
                kv = kev[a % 2]
                p.step("ACT", lambda e, o=kv, i=pa: e.activation(o, i, AF.Copy),
                       reads=[bA[a % 2]], writes=[b_kev[a % 2]])
                p.dma("ACT", f"kev{a % 2}", lambda e, a=a, s=s, i=kv: e.dma_start(
                    out=ktS[a][:, 512 * s:512 * (s + 1)], in_=i),
                    reads=[b_kev[a % 2]], writes=[b_ktS])
            if s + 1 < NSTRIP:
                p1_ln(s + 1)
            for t in range(4):
                til = 4 * s + t
                odd = (t % 2 == 0)
                for c in range(NCH):
                    for fi, (fo, fw) in enumerate(_splits(E)):
                        if odd:
                            if fo < 1024:
                                pv = psA[:, fo:fo + fw]
                                wb_ = bA[fi % 2]
                            else:
                                pv = psN[:, fo - 1024:fo - 1024 + fw]
                                wb_ = bT[fi % 2]
                        else:
                            pv = psB[:, fo:fo + fw]
                            wb_ = bB[(2 * fo) // E]
                        p.step("PE", lambda e, pv=pv, c=c, t=t, fo=fo, fw=fw,
                               st=(c == 0), sp=(c == NCH - 1):
                               e.matmul(pv, xnT[:, 512 * c + 128 * t:512 * c + 128 * t + 128],
                                        W2[:, E * c + fo:E * c + fo + fw], start=st, stop=sp),
                               reads=[b_xnT[c // 4], b_W2], writes=[wb_])
                if odd:
                    p.step("DVE", lambda e: e.tensor_copy(vev[:, 0:min(E, 1024)], psA[:, 0:min(E, 1024)]),
                           reads=[bA[0], bA[1]], writes=[b_vev])
                    if E > 1024:
                        p.step("DVE", lambda e: e.tensor_copy(vev[:, 1024:E], psN[:, 0:E - 1024]),
                               reads=[bT[0], bT[1]], writes=[b_vev])
                else:
                    p.step("DVE", lambda e: e.tensor_copy(vev, psB[:, 0:E]),
                           reads=[bB[0], bB[1]], writes=[b_vev])
                p.dma("ACT", "vev", lambda e, til=til: e.dma_start(
                    out=vS[128 * til:128 * (til + 1), :], in_=vev),
                    reads=[b_vev], writes=[b_vS])
            if s + 1 < NSTRIP:
                p1_transposes(s + 1)
        al.pop()
        p.barrier()

        al.push()
        o_wq = al.take(NCH * E // 2)
        o_xd2 = [al.take(E), al.take(E)]
        o_xnb4 = al.take(4 * E // 2)
        o_nxb4 = al.take(4 * E // 2)
        o_xnT2 = al.take(NCH * 512 // 2)
        o_qev = al.take(512)
        o_ci2 = al.take(E)
        o_f2 = al.take(E)
        o_r12 = al.take(E)
        o_nxT = al.take(E)
        WQ = cb(o_wq, NCH * E)
        xd2 = [cf(o_xd2[0], E), cf(o_xd2[1], E)]
        xnb4 = cb(o_xnb4, 4 * E)
        nxb4 = cb(o_nxb4, 4 * E)
        xnT2 = cb(o_xnT2, NCH * 512)
        xnT23 = xnT2.rearrange("p (c k) -> p c k", k=512)
        qev = [cb(o_qev, 512), cb(o_qev + 256, 512)]
        ci2 = cI(o_ci2, E)
        f2 = cf(o_f2, E)
        r12 = cf(o_r12, E)
        nxT = [cb(o_nxT, E), cb(o_nxT + E // 2, E)]
        b_WQ = Buf("WQ", slices=True)
        b_xd2 = [Buf("x2a"), Buf("x2b")]
        b_xnb24 = [Buf(f"xnb24_{i}") for i in range(4)]
        b_nxb4 = [Buf(f"nxb4_{i}") for i in range(4)]
        b_xnT21 = Buf("xnT2")
        b_xnT2 = [b_xnT21] * (NCH // 4)
        b_qev = [Buf("qev0"), Buf("qev1")]
        b_ci2 = Buf("ci2")
        b_f2 = Buf("f2")
        b_r12 = Buf("r12")
        b_nxT = [Buf("nxT0"), Buf("nxT1")]

        for c in range(NCH):
            p.dma("GPQ", "wload", lambda e, c=c: e.dma_start(
                out=WQ[:, E * c:E * (c + 1)], in_=wqT[128 * c:128 * (c + 1), :]),
                writes=[b_WQ])

        def p2_ln(so):
            for t in range(4):
                slot = 4 * so + t
                sl = slot % 2
                xn_t = xnb4[:, E * t:E * (t + 1)]
                nx_t = nxb4[:, E * t:E * (t + 1)]
                p.dma("SYNC", f"x{sl}", lambda e, slot=slot, sl=sl: e.dma_start(
                    out=xd2[sl], in_=x_own[128 * slot:128 * (slot + 1), :]),
                    writes=[b_xd2[sl]])
                rr, negmr, mm = ln_tile(xd2[sl], b_xd2[sl], xn_t, b_xnb24[t], xn_t, b_xnb24[t], t)
                qg = 4 + t
                bqg = b_tiny[qg]
                am, fx, am2, r127, rrt, biasq, thr = (tiny(qg, i) for i in range(7))
                p.step("DVE", lambda e, am=am, xn_t=xn_t, t=t: e.reduce_max(
                    am, xn_t, axis=AX.X, apply_absolute_value=True),
                    reads=[b_xnb24[t]], writes=[bqg])
                p.step("ACT", lambda e, fx=fx, am=am: e.activation(fx, am, AF.Sign), reads=[bqg], writes=[bqg])
                p.step("DVE", lambda e, fx=fx: e.tensor_scalar(fx, fx, -1.0, 1.0, op0=ALU.mult, op1=ALU.add),
                       reads=[bqg], writes=[bqg])
                p.step("DVE", lambda e, am2=am2, am=am, fx=fx: e.tensor_add(am2, am, fx), reads=[bqg], writes=[bqg])
                p.step("DVE", lambda e, r127=r127, am2=am2: e.reciprocal(r127, am2), reads=[bqg], writes=[bqg])
                p.step("DVE", lambda e, r127=r127: e.tensor_scalar(r127, r127, 127.0, None, op0=ALU.mult),
                       reads=[bqg], writes=[bqg])
                p.step("DVE", lambda e, slot=slot, am2=am2: e.tensor_scalar(
                    srowP[:, slot:slot + 1], am2, 1.0 / 127.0, None, op0=ALU.mult),
                    reads=[bqg], writes=[b_srow])
                p.step("DVE", lambda e, rr=rr, rrt=rrt, r127=r127: e.tensor_tensor(rrt, rr, r127, op=ALU.mult),
                       reads=[b_tiny[t], bqg], writes=[bqg])
                p.step("DVE", lambda e, mm=mm, biasq=biasq, rrt=rrt: e.tensor_scalar(
                    biasq, mm, rrt, -1.0, op0=ALU.mult, op1=ALU.mult),
                    reads=[b_tiny[t], bqg], writes=[bqg])
                p.step("DVE", lambda e, biasq=biasq: e.tensor_scalar(biasq, biasq, 128.0, None, op0=ALU.add),
                       reads=[bqg], writes=[bqg])
                thr = tiny(qg, 6)
                p.step("DVE", lambda e, thr=thr, biasq=biasq: e.tensor_scalar(
                    thr, biasq, -1.0, 0.5, op0=ALU.mult, op1=ALU.add),
                    reads=[bqg], writes=[bqg])
                p.step("ACT", lambda e, x=xd2[sl], rrt=rrt, biasq=biasq: e.activation(
                    ci2, x, AF.Identity, scale=rrt, bias=biasq),
                    reads=[b_xd2[sl], bqg], writes=[b_ci2])
                p.step("DVE", lambda e, x=xd2[sl], rrt=rrt: e.scalar_tensor_tensor(
                    f2, x, rrt, ci2, op0=ALU.mult, op1=ALU.subtract),
                    reads=[b_xd2[sl], bqg, b_ci2], writes=[b_f2])
                p.step("DVE", lambda e, thr=thr: e.scalar_tensor_tensor(
                    r12, f2, thr, ci2, op0=ALU.is_gt, op1=ALU.add),
                    reads=[b_f2, bqg, b_ci2], writes=[b_r12])
                p.step("ACT", lambda e, nx_t=nx_t, t=t: e.activation(nx_t, r12, AF.Copy, bias=-128.0),
                       reads=[b_r12], writes=[b_nxb4[t]])

        def p2_transposes(so):
            for t in range(4):
                slot = 4 * so + t
                nsl = t % 2
                transpose_chunks(nxb4[:, E * t:E * (t + 1)], b_nxb4[t], NCH,
                                 lambda j, nsl=nsl: (nxT[nsl][:, 512 * j:512 * j + 512], b_nxT[nsl]),
                                 evac_cls="ACT", par=t)
                nxT3 = nxT[nsl].rearrange("p (c j) -> p c j", j=128)
                p.dma("ACT", "nxSp", lambda e, slot=slot, nxT3=nxT3: e.dma_start(
                    out=nxS[slot].rearrange("c p j -> p c j"), in_=nxT3),
                    reads=[b_nxT[nsl]], writes=[b_nxS])
                transpose_chunks(
                    xnb4[:, E * t:E * (t + 1)], b_xnb24[t], NCH,
                    lambda j, t=t: (xnT23[:, 4 * j:4 * j + 4, 128 * t:128 * t + 128],
                                    b_xnT2[j]), par=t + 1)

        p2_ln(0)
        p2_transposes(0)
        for so in range(NSO):
            for a in range(NCH):
                pa = psA[:, 512 * (a % 2):512 * (a % 2) + 512]
                for c in range(NCH):
                    p.step("PE", lambda e, pa=pa, a=a, c=c, st=(c == 0), sp=(c == NCH - 1):
                           e.matmul(pa, WQ[:, E * c + 128 * a:E * c + 128 * a + 128],
                                    xnT2[:, 512 * c:512 * c + 512], start=st, stop=sp),
                           reads=[b_WQ, b_xnT2[c // 4]], writes=[bA[a % 2]])
                qv = qev[a % 2]
                p.step("ACT", lambda e, o=qv, i=pa: e.activation(o, i, AF.Copy, scale=RSQ),
                       reads=[bA[a % 2]], writes=[b_qev[a % 2]])
                p.dma("ACT", f"qev{a % 2}", lambda e, so=so, a=a, i=qv: e.dma_start(
                    out=qtS[a, so], in_=i),
                    reads=[b_qev[a % 2]], writes=[b_qtS])
            if so + 1 < NSO:
                p2_ln(so + 1)
                p2_transposes(so + 1)
        al.pop()
        p.barrier()

        al.push()
        o_nk = al.take(NCH * E // 2)
        o_bqb = al.take(E)
        NK = cb(o_nk, NCH * E)
        bqb = cf(o_bqb, E)
        b_NK = [Buf(f"NK_{i}") for i in range(NCH)]
        b_bqb = Buf("bqb")

        al.push()
        o_fk = [al.take(E), al.take(E)]
        o_tab = al.take(E)
        o_rcb = al.take(E)
        o_scb = al.take(E)
        o_scp = al.take(NCH)
        o_ar = al.take(128)
        o_row0 = al.take(E)
        o_row1 = al.take(E)
        o_row2 = al.take(E)
        o_ci3 = al.take(E)
        o_f3 = al.take(E)
        o_r13 = al.take(E)
        o_cirow = o_ci3
        o_M = o_f3
        fk = [cf(o_fk[0], E), cf(o_fk[1], E)]
        Mt = cf(o_M, E)
        tab = cf(o_tab, E)
        rcb = cf(o_rcb, E)
        scb = cf(o_scb, E)
        scp = cf(o_scp, NCH)
        arow = cf(o_ar, 128)
        row0 = cf(o_row0, E)
        row1 = cf(o_row1, E)
        row2 = cf(o_row2, E)
        cirow = cI(o_cirow, E)
        ci3 = cI(o_ci3, E)
        f3 = cf(o_f3, E)
        r13 = cf(o_r13, E)
        b_fk = [Buf("fk0"), Buf("fk1")]
        b_tab = Buf("tab")
        b_rcb = Buf("rcb")
        b_scb = Buf("scb")
        b_scp = Buf("scp")
        b_ar = Buf("arow")
        b_row = [Buf("row0"), Buf("row1"), Buf("row2")]
        b_ci3 = Buf("ci3")
        b_cirow = b_ci3
        b_f3 = Buf("f3")
        b_M = b_f3
        b_r13 = Buf("r13")

        for c in range(NCH):
            sl = c % 2
            p.dma("SYNC", f"fk{sl}", lambda e, c=c, sl=sl: e.dma_start(
                out=fk[sl], in_=fker[128 * c:128 * (c + 1), :]), writes=[b_fk[sl]])
            p.step("ACT", lambda e, sl=sl: e.activation(tab, fk[sl], AF.Abs),
                   reads=[b_fk[sl]], writes=[b_tab])
            if c == 0:
                p.step("DVE", lambda e: e.tensor_copy(Mt, tab), reads=[b_tab], writes=[b_M])
            else:
                p.step("DVE", lambda e: e.tensor_max(Mt, Mt, tab), reads=[b_tab, b_M], writes=[b_M])
        for t in range(NCH):
            pslot = psF[:, 0:128]
            p.step("PE", lambda e, o=pslot, t=t: e.transpose(o, Mt[:, 128 * t:128 * t + 128], identf),
                   reads=[b_M, b_identf], writes=[bF[0]])
            p.step("DVE", lambda e, i=pslot, t=t: e.reduce_max(scp[:, t:t + 1], i, axis=AX.X),
                   reads=[bF[0]], writes=[b_scp])
        p.step("PE", lambda e: e.transpose(psF[0:NCH, 0:128], scp, identf),
               reads=[b_scp, b_identf], writes=[bF[0]])
        p.step("DVE", lambda e: e.tensor_copy(arow[0:NCH, :], psF[0:NCH, 0:128]),
               reads=[bF[0]], writes=[b_ar])
        p.dma("GPQ", "rowW", lambda e: e.dma_start(
            out=rowS.rearrange("o (a b) -> (o a) b", a=NCH, b=128), in_=arow[0:NCH, :]),
            reads=[b_ar], writes=[b_rowS])
        p.dma("GPQ", "rowW", lambda e: e.dma_start(out=row0[0:1, :], in_=rowS),
               reads=[b_rowS], writes=[b_row[0]])
        p.step("ACT", lambda e: e.sign(row1[0:1, :], row0[0:1, :]),
               reads=[b_row[0]], writes=[b_row[1]])
        p.step("DVE", lambda e: e.tensor_scalar(row1[0:1, :], row1[0:1, :], -1.0, 1.0,
                                                op0=ALU.mult, op1=ALU.add),
               reads=[b_row[1]], writes=[b_row[1]])
        p.step("DVE", lambda e: e.tensor_add(row0[0:1, :], row0[0:1, :], row1[0:1, :]),
               reads=[b_row[0], b_row[1]], writes=[b_row[0]])
        p.step("DVE", lambda e: e.reciprocal(row2[0:1, :], row0[0:1, :]),
               reads=[b_row[0]], writes=[b_row[2]])
        p.step("DVE", lambda e: e.tensor_scalar(row2[0:1, :], row2[0:1, :], 127.0, None, op0=ALU.mult),
               reads=[b_row[2]], writes=[b_row[2]])
        p.step("DVE", lambda e: e.tensor_scalar(row1[0:1, :], row0[0:1, :], 1.0 / 127.0, None,
                                                op0=ALU.mult),
               reads=[b_row[0]], writes=[b_row[1]])
        for (srcap, bsrc, dst, bdst) in ((row2, b_row[2], rcb, b_rcb), (row1, b_row[1], scb, b_scb)):
            for (fo, fw) in _splits(E):
                pa = psA[:, 0:fw]
                p.step("PE", lambda e, pa=pa, srcap=srcap, fo=fo, fw=fw: e.matmul(
                    pa, ones_f[0:1, 0:128], srcap[0:1, fo:fo + fw], start=True, stop=True),
                    reads=[bsrc, b_ones], writes=[bA[0]])
                p.step("DVE", lambda e, pa=pa, dst=dst, fo=fo, fw=fw: e.tensor_copy(
                    dst[:, fo:fo + fw], pa), reads=[bA[0]], writes=[bdst])
        p.dma("SYNC", "miscS", lambda e: e.dma_start(out=row0[0:1, :], in_=fbias),
               reads=[b_row[0]], writes=[b_row[0]])
        tb0, tb1, tb2 = tiny(7, 0), tiny(7, 1), tiny(7, 2)
        btb = b_tiny[7]
        p.step("DVE", lambda e: e.reduce_max(tb0[0:1, :], row0[0:1, :], axis=AX.X,
                                             apply_absolute_value=True),
               reads=[b_row[0]], writes=[btb])
        p.step("ACT", lambda e: e.sign(tb1[0:1, :], tb0[0:1, :]), reads=[btb], writes=[btb])
        p.step("DVE", lambda e: e.tensor_scalar(tb1[0:1, :], tb1[0:1, :], -1.0, 1.0,
                                                op0=ALU.mult, op1=ALU.add),
               reads=[btb], writes=[btb])
        p.step("DVE", lambda e: e.tensor_add(tb0[0:1, :], tb0[0:1, :], tb1[0:1, :]),
               reads=[btb], writes=[btb])
        p.step("DVE", lambda e: e.reciprocal(tb1[0:1, :], tb0[0:1, :]), reads=[btb], writes=[btb])
        p.step("DVE", lambda e: e.tensor_scalar(tb1[0:1, :], tb1[0:1, :], 127.0, None, op0=ALU.mult),
               reads=[btb], writes=[btb])
        p.step("DVE", lambda e: e.tensor_scalar(tb2[0:1, :], tb0[0:1, :], 1.0 / 127.0, None,
                                                op0=ALU.mult),
               reads=[btb], writes=[btb])
        p.step("ACT", lambda e: e.activation(cirow[0:1, :], row0[0:1, :], AF.Copy,
                                             scale=tb1[0:1, :], bias=128.0),
               reads=[b_row[0], btb], writes=[b_cirow])
        p.step("DVE", lambda e: e.scalar_tensor_tensor(
            row2[0:1, :], row0[0:1, :], tb1[0:1, :], cirow[0:1, :],
            op0=ALU.mult, op1=ALU.subtract),
            reads=[b_row[0], btb, b_cirow], writes=[b_row[2]])
        p.step("DVE", lambda e: e.scalar_tensor_tensor(
            row1[0:1, :], row2[0:1, :], -127.5, cirow[0:1, :], op0=ALU.is_gt, op1=ALU.add),
            reads=[b_row[2], b_cirow], writes=[b_row[1]])
        p.step("DVE", lambda e: e.tensor_scalar(row1[0:1, :], row1[0:1, :], 128.0, tb2[0:1, :],
                                                op0=ALU.subtract, op1=ALU.mult),
               reads=[b_row[1], btb], writes=[b_row[1]])
        for (fo, fw) in _splits(E):
            pa = psA[:, 0:fw]
            p.step("PE", lambda e, pa=pa, fo=fo, fw=fw: e.matmul(
                pa, ones_f[0:1, 0:128], row1[0:1, fo:fo + fw], start=True, stop=True),
                reads=[b_row[1], b_ones], writes=[bA[0]])
            p.step("DVE", lambda e, pa=pa, fo=fo, fw=fw: e.tensor_copy(
                bqb[:, fo:fo + fw], pa), reads=[bA[0]], writes=[b_bqb])
        for c in range(NCH):
            sl = c % 2
            p.dma("SYNC", f"fk{sl}", lambda e, c=c, sl=sl: e.dma_start(
                out=fk[sl], in_=fker[128 * c:128 * (c + 1), :]),
                reads=[b_fk[sl]], writes=[b_fk[sl]])
            p.step("POOL", lambda e, sl=sl: e.tensor_mul(tab, fk[sl], rcb),
                   reads=[b_fk[sl], b_rcb], writes=[b_tab])
            p.step("ACT", lambda e: e.activation(ci3, tab, AF.Copy, bias=128.0),
                   reads=[b_tab], writes=[b_ci3])
            p.step("DVE", lambda e: e.scalar_tensor_tensor(
                f3, tab, 128.0, ci3, op0=ALU.add, op1=ALU.subtract),
                reads=[b_tab, b_ci3], writes=[b_f3])
            p.step("DVE", lambda e: e.scalar_tensor_tensor(
                r13, f3, 0.5, ci3, op0=ALU.is_gt, op1=ALU.add),
                reads=[b_f3, b_ci3], writes=[b_r13])
            p.step("DVE", lambda e, c=c: e.scalar_tensor_tensor(
                NK[:, E * c:E * (c + 1)], r13, 128.0, scb, op0=ALU.subtract, op1=ALU.mult),
                reads=[b_r13, b_scb], writes=[b_NK[c]])
        al.pop()
        p.barrier()

        al.push()
        o_nxt = [al.take(E // 2), al.take(E // 2)]
        o_y2 = al.take(E)
        o_y3 = al.take(E)
        o_ciy = al.take(E)
        o_fy = al.take(E)
        o_r1y = al.take(E)
        o_nyb = al.take(E // 2)
        nxt = [cb(o_nxt[0], E), cb(o_nxt[1], E)]
        y2 = cf(o_y2, E)
        y3 = cf(o_y3, E)
        ciy = cI(o_ciy, E)
        fy = cf(o_fy, E)
        r1y = cf(o_r1y, E)
        nyb = cb(o_nyb, E)
        b_nxt = [Buf("nxt0", slices=True), Buf("nxt1", slices=True)]
        b_y2 = Buf("y2")
        b_y3 = Buf("y3")
        b_ciy = Buf("ciy")
        b_fy = Buf("fy")
        b_r1y = Buf("r1y")
        b_nyb = Buf("nyb")

        for slot in range(OWNT):
            sl = slot % 2
            p.dma("SYNC", f"nxt{sl}", lambda e, slot=slot, sl=sl: e.dma_start(
                out=nxt[sl].rearrange("p (c j) -> p c j", j=128),
                in_=nxS[slot].rearrange("c p j -> p c j")),
                reads=[b_nxS], writes=[b_nxt[sl]])
            odd = slot % 2
            for c in range(NCH):
                for fi, (fo, fw) in enumerate(_splits(E)):
                    if odd:
                        if fo < 1024:
                            pv = psA[:, fo:fo + fw]
                            wb_ = bA[fi % 2]
                        else:
                            pv = psN[:, fo - 1024:fo - 1024 + fw]
                            wb_ = bT[fi % 2]
                    else:
                        pv = psB[:, fo:fo + fw]
                        wb_ = bB[(2 * fo) // E]
                    p.step("PE", lambda e, pv=pv, c=c, sl=sl, fo=fo, fw=fw,
                           st=(c == 0), sp=(c == NCH - 1): e.matmul(
                        pv, nxt[sl][:, 128 * c:128 * c + 128],
                        NK[:, E * c + fo:E * c + fo + fw], start=st, stop=sp),
                        reads=[b_nxt[sl], b_NK[c]], writes=[wb_])
            fg = 6 + sl
            bfg = b_tiny[fg]
            am, am2, ry = tiny(fg, 0), tiny(fg, 1), tiny(fg, 2)
            if odd:
                p.step("DVE", lambda e, slot=slot: e.scalar_tensor_tensor(
                    y2[:, 0:min(E, 1024)], psA[:, 0:min(E, 1024)], srowP[:, slot:slot + 1],
                    bqb[:, 0:min(E, 1024)], op0=ALU.mult, op1=ALU.add),
                    reads=[bA[0], bA[1], b_srow, b_bqb], writes=[b_y2])
                if E > 1024:
                    p.step("DVE", lambda e, slot=slot: e.scalar_tensor_tensor(
                        y2[:, 1024:E], psN[:, 0:E - 1024], srowP[:, slot:slot + 1],
                        bqb[:, 1024:E], op0=ALU.mult, op1=ALU.add),
                        reads=[bT[0], bT[1], b_srow, b_bqb], writes=[b_y2])
            else:
                p.step("DVE", lambda e, slot=slot: e.scalar_tensor_tensor(
                    y2, psB[:, 0:E], srowP[:, slot:slot + 1], bqb, op0=ALU.mult, op1=ALU.add),
                    reads=[bB[0], bB[1], b_srow, b_bqb], writes=[b_y2])
            p.step("DVE", lambda e, am=am: e.reduce_max(am, y2, axis=AX.X), reads=[b_y2], writes=[bfg])
            p.step("DVE", lambda e, am=am: e.tensor_scalar(am, am, 0.0, None, op0=ALU.max),
                   reads=[bfg], writes=[bfg])
            p.step("ACT", lambda e, am2=am2, am=am: e.sign(am2, am), reads=[bfg], writes=[bfg])
            p.step("DVE", lambda e, am2=am2: e.tensor_scalar(am2, am2, -1.0, 1.0, op0=ALU.mult, op1=ALU.add),
                   reads=[bfg], writes=[bfg])
            p.step("DVE", lambda e, am2=am2, am=am: e.tensor_add(am2, am, am2), reads=[bfg], writes=[bfg])
            p.step("DVE", lambda e, ry=ry, am2=am2: e.reciprocal(ry, am2), reads=[bfg], writes=[bfg])
            p.step("DVE", lambda e, ry=ry: e.tensor_scalar(ry, ry, 127.0, None, op0=ALU.mult),
                   reads=[bfg], writes=[bfg])
            p.step("DVE", lambda e, slot=slot, am2=am2: e.tensor_scalar(
                syP[:, slot:slot + 1], am2, 1.0 / 127.0, None, op0=ALU.mult),
                reads=[bfg], writes=[b_sy])
            p.step("ACT", lambda e, ry=ry: e.activation(y3, y2, AF.Relu, scale=ry),
                   reads=[b_y2, bfg], writes=[b_y3])
            p.step("ACT", lambda e: e.activation(ciy, y3, AF.Copy), reads=[b_y3], writes=[b_ciy])
            p.step("DVE", lambda e: e.tensor_sub(fy, y3, ciy), reads=[b_y3, b_ciy], writes=[b_fy])
            p.step("DVE", lambda e: e.scalar_tensor_tensor(
                r1y, fy, 0.5, ciy, op0=ALU.is_gt, op1=ALU.add),
                reads=[b_fy, b_ciy], writes=[b_r1y])
            p.step("DVE", lambda e: e.tensor_copy(nyb, r1y), reads=[b_r1y], writes=[b_nyb])
            p.dma("ACT", "nySp", lambda e, slot=slot: e.dma_start(
                out=nyS[128 * slot:128 * (slot + 1), :], in_=nyb),
                reads=[b_nyb], writes=[b_nyS])
        al.pop()
        al.pop()
        p.barrier()

        al.push()
        o_SR = [al.take(S), al.take(S)]
        o_Pb = al.take(S // 2)
        o_PT = [al.take(S // 2), al.take(S // 2)]
        o_ktb = [al.take(NCH * 512 // 2), al.take(NCH * 512 // 2)]
        o_vt = [al.take(EH // 2), al.take(EH // 2)]
        o_qt = [[al.take(NCH * 128 // 2), al.take(NCH * 128 // 2)],
                [al.take(NCH * 128 // 2), al.take(NCH * 128 // 2)]]
        o_lp = al.take(2 * max(NB, 8))
        o_x4 = [al.take(E), al.take(E)]
        o_ny4 = [al.take(E // 2), al.take(E // 2)]
        o_ost = [al.take(E), al.take(E)]
        o_ci4 = al.take(EH)
        o_f4 = al.take(EH)
        o_r14 = al.take(EH)
        SR = [cf(o_SR[0], S), cf(o_SR[1], S)]
        Pb = cb(o_Pb, S)
        PT = [cb(o_PT[0], S), cb(o_PT[1], S)]
        ktb = [cb(o_ktb[0], NCH * 512), cb(o_ktb[1], NCH * 512)]
        vt = [cb(o_vt[0], EH), cb(o_vt[1], EH)]
        qt = [[cb(o_qt[0][0], NCH * 128), cb(o_qt[0][1], NCH * 128)],
              [cb(o_qt[1][0], NCH * 128), cb(o_qt[1][1], NCH * 128)]]
        lp = [cf(o_lp, NB), cf(o_lp + NB, NB)]
        x4 = [cf(o_x4[0], E), cf(o_x4[1], E)]
        ny4 = [cb(o_ny4[0], E), cb(o_ny4[1], E)]
        ost = [cf(o_ost[0], E), cf(o_ost[1], E)]
        ci4 = cI(o_ci4, EH)
        f4 = cf(o_f4, EH)
        r14 = cf(o_r14, EH)
        b_SR = [[Buf(f"SR0_{i}") for i in range(NB)], [Buf(f"SR1_{i}") for i in range(NB)]]
        b_Pb = [Buf(f"Pb{i}") for i in range(S // 512)]
        b_PT = [[Buf(f"PT0_{i}") for i in range(S // 512)],
                [Buf(f"PT1_{i}") for i in range(S // 512)]]
        b_ktb = [Buf("ktb0", slices=True), Buf("ktb1", slices=True)]
        b_vt = [Buf("vt0"), Buf("vt1")]
        b_qt = [[Buf("qt00"), Buf("qt01")], [Buf("qt10"), Buf("qt11")]]
        b_lp = [Buf("lp0"), Buf("lp1")]
        b_x4 = [Buf("x40"), Buf("x41")]
        b_ny4 = [Buf("ny40"), Buf("ny41")]
        b_ost = [Buf("ost0"), Buf("ost1")]
        b_ci4 = Buf("ci4")
        b_f4 = Buf("f4")
        b_r14 = Buf("r14")

        def load_qt(g):
            par = g % 2
            for T in range(2):
                slot = 2 * g + T
                so, tt = slot // 4, slot % 4
                p.dma("SYNC", f"qt{par}{T}", lambda e, so=so, tt=tt, par=par, T=T: e.dma_start(
                    out=qt[par][T].rearrange("p (a j) -> p a j", j=128),
                    in_=qtS[:, so, :, 128 * tt:128 * tt + 128].rearrange("a p j -> p a j")),
                    reads=[b_qtS], writes=[b_qt[par][T]])

        load_qt(0)
        for g in range(NB):
            KP = 512 * (g + 1)
            nkc = KP // 128
            par = g % 2
            if g + 1 < NB:
                load_qt(g + 1)
            for T in range(2):
                slot = 2 * g + T
                p.dma("SYNC", f"x4{T}", lambda e, slot=slot, T=T: e.dma_start(
                    out=x4[T], in_=x_own[128 * slot:128 * (slot + 1), :]), writes=[b_x4[T]])
                p.dma("SYNC", f"ny4{T}", lambda e, slot=slot, T=T: e.dma_start(
                    out=ny4[T], in_=nyS[128 * slot:128 * (slot + 1), :]),
                    reads=[b_nyS], writes=[b_ny4[T]])
            def load_ktb(s):
                ks = s % 2
                p.dma("SYNC", f"ktb{ks}", lambda e, s=s, ks=ks: e.dma_start(
                    out=ktb[ks].rearrange("p (c j) -> p c j", j=512),
                    in_=ktS[:, :, 512 * s:512 * (s + 1)].rearrange("c p j -> p c j")),
                    reads=[b_ktS], writes=[b_ktb[ks]])

            for s in range(g + 1):
                ks = s % 2
                if s > 0 or g == 0:
                    load_ktb(s)
                for T in range(2):
                    sg = 4 + T
                    bsg = b_tiny[sg]
                    rm, tmpm = tiny(sg, 0), tiny(sg, 3)
                    pa = psA[:, 512 * T:512 * T + 512]
                    for c in range(NCH):
                        p.step("PE", lambda e, pa=pa, T=T, c=c, ks=ks, par=par,
                               st=(c == 0), sp=(c == NCH - 1): e.matmul(
                            pa, qt[par][T][:, 128 * c:128 * c + 128],
                            ktb[ks][:, 512 * c:512 * (c + 1)], start=st, stop=sp),
                            reads=[b_qt[par][T], b_ktb[ks]], writes=[bA[T]])
                    if s == g:
                        p.step("DVE", lambda e, T=T, s=s, pa=pa: e.tensor_tensor(
                            SR[T][:, 512 * s:512 * (s + 1)], pa,
                            mask_sb[:, 512 * T:512 * (T + 1)], op=ALU.add),
                            reads=[bA[T], b_masks], writes=[b_SR[T][s]])
                    else:
                        p.step("ACT", lambda e, T=T, s=s, pa=pa: e.activation(
                            SR[T][:, 512 * s:512 * (s + 1)], pa, AF.Copy),
                            reads=[bA[T]], writes=[b_SR[T][s]])
                    dstm = rm if s == 0 else tmpm
                    p.step("DVE", lambda e, T=T, s=s, dstm=dstm: e.reduce_max(
                        dstm, SR[T][:, 512 * s:512 * (s + 1)], axis=AX.X),
                        reads=[b_SR[T][s]], writes=[bsg])
                    if s > 0:
                        p.step("DVE", lambda e, rm=rm, tmpm=tmpm: e.tensor_max(rm, rm, tmpm),
                               reads=[bsg], writes=[bsg])
            if g + 1 < NB:
                load_ktb(0)
            for T in range(2):
                slot = 2 * g + T
                p.step("DVE", lambda e, T=T, slot=slot: e.tensor_scalar(
                    ost[T], ny4[T], syP[:, slot:slot + 1], -AOFF_SC, op0=ALU.mult, op1=ALU.add),
                    reads=[b_ny4[T], b_sy], writes=[b_ost[T]])
                p.step("POOL", lambda e, T=T: e.tensor_add(ost[T], ost[T], x4[T]),
                       reads=[b_ost[T], b_x4[T]], writes=[b_ost[T]])
            for T in range(2):
                sg = 4 + T
                bsg = b_tiny[sg]
                rm, negm = tiny(sg, 0), tiny(sg, 1)
                p.step("DVE", lambda e, rm=rm, negm=negm: e.tensor_scalar(
                    negm, rm, -1.0, None, op0=ALU.mult), reads=[bsg], writes=[bsg])
            for j in range(g + 1):
                for T in range(2):
                    sg = 4 + T
                    bsg = b_tiny[sg]
                    negm = tiny(sg, 1)
                    p.step("ACT", lambda e, T=T, j=j, negm=negm: e.activation(
                        Pb[:, 512 * j:512 * (j + 1)], SR[T][:, 512 * j:512 * (j + 1)],
                        AF.Exp, bias=negm, accum_out=lp[T][:, j:j + 1]),
                        reads=[b_SR[T][j], bsg], writes=[b_Pb[j], b_lp[T]])
                    transpose_chunks(Pb[:, 512 * j:512 * (j + 1)], b_Pb[j], 4,
                                     lambda jj, T=T, j=j: (PT[T][:, 512 * j:512 * j + 512],
                                                           b_PT[T][j]), par=2 * j + T)
                for kc in range(4 * j, 4 * j + 4):
                    vs_ = kc % 2
                    p.dma("SYNC", f"vt{vs_}", lambda e, kc=kc, vs_=vs_: e.dma_start(
                        out=vt[vs_], in_=vS[128 * kc:128 * (kc + 1), 0:EH]),
                        reads=[b_vS], writes=[b_vt[vs_]])
                    for T in range(2):
                        for (fo, fw) in _splits(EH):
                            pv = psB[:, PVW * T + fo:PVW * T + fo + fw]
                            p.step("PE", lambda e, pv=pv, T=T, kc=kc, vs_=vs_, fo=fo, fw=fw,
                                   st=(kc == 0), sp=(kc == nkc - 1): e.matmul(
                                pv, PT[T][:, 128 * kc:128 * kc + 128],
                                vt[vs_][:, fo:fo + fw], start=st, stop=sp),
                                reads=[b_PT[T][kc // 4], b_vt[vs_]], writes=[bB[T]])
            for T in range(2):
                sg = 4 + T
                bsg = b_tiny[sg]
                rsc, lsum = tiny(sg, 2), tiny(sg, 4)
                p.step("DVE", lambda e, T=T, lsum=lsum, g=g: e.reduce_sum(
                    lsum, lp[T][:, 0:g + 1], axis=AX.X), reads=[b_lp[T]], writes=[bsg])
                p.step("DVE", lambda e, rsc=rsc, lsum=lsum: e.reciprocal(rsc, lsum),
                       reads=[bsg], writes=[bsg])
                p.step("DVE", lambda e, rsc=rsc: e.tensor_scalar(rsc, rsc, 1.0e4, None, op0=ALU.mult),
                       reads=[bsg], writes=[bsg])
            for sweep in range(2):
                if sweep == 1:
                    for kc in range(nkc):
                        vs_ = kc % 2
                        p.dma("SYNC", f"vt{vs_}", lambda e, kc=kc, vs_=vs_: e.dma_start(
                            out=vt[vs_], in_=vS[128 * kc:128 * (kc + 1), EH:E]),
                            reads=[b_vS], writes=[b_vt[vs_]])
                        for T in range(2):
                            for (fo, fw) in _splits(EH):
                                if T == 0 and PVW <= 512:
                                    pv = psA[:, fo:fo + fw]
                                    wb_ = bA[0]
                                elif T == 1 and PVW <= 512:
                                    pv = psA[:, 512 + fo:512 + fo + fw]
                                    wb_ = bA[1]
                                elif T == 0:
                                    pv = psA[:, fo:fo + fw]
                                    wb_ = bA[fo // 512]
                                else:
                                    pv = psN[:, fo:fo + fw]
                                    wb_ = bT[fo // 512]
                                p.step("PE", lambda e, pv=pv, T=T, kc=kc, vs_=vs_, fo=fo, fw=fw,
                                       st=(kc == 0), sp=(kc == nkc - 1): e.matmul(
                                    pv, PT[T][:, 128 * kc:128 * kc + 128],
                                    vt[vs_][:, fo:fo + fw], start=st, stop=sp),
                                    reads=[b_PT[T][kc // 4], b_vt[vs_]], writes=[wb_])
                for T in range(2):
                    sg = 4 + T
                    bsg = b_tiny[sg]
                    rsc = tiny(sg, 2)
                    if sweep == 0:
                        pvh = psB[:, PVW * T:PVW * T + EH]
                        rdb = [bB[T]]
                    elif PVW <= 512:
                        pvh = psA[:, 512 * T:512 * T + EH]
                        rdb = [bA[T]]
                    elif T == 0:
                        pvh = psA[:, 0:EH]
                        rdb = [bA[0], bA[1]]
                    else:
                        pvh = psN[:, 0:EH]
                        rdb = [bT[0], bT[1]]
                    p.step("ACT", lambda e, pvh=pvh, rsc=rsc: e.activation(
                        ci4, pvh, AF.Copy, scale=rsc, bias=AOFF),
                        reads=rdb + [bsg], writes=[b_ci4])
                    p.step("DVE", lambda e, pvh=pvh, rsc=rsc: e.scalar_tensor_tensor(
                        f4, pvh, rsc, ci4, op0=ALU.mult, op1=ALU.subtract),
                        reads=rdb + [bsg, b_ci4], writes=[b_f4])
                    p.step("DVE", lambda e: e.scalar_tensor_tensor(
                        r14, f4, 0.5 - AOFF, ci4, op0=ALU.is_gt, op1=ALU.add),
                        reads=[b_f4, b_ci4], writes=[b_r14])
                    osl = ost[T][:, EH * sweep:EH * (sweep + 1)]
                    p.step("DVE", lambda e, osl=osl: e.scalar_tensor_tensor(
                        osl, r14, 1.0e-4, osl, op0=ALU.mult, op1=ALU.add),
                        reads=[b_r14, b_ost[T]], writes=[b_ost[T]])
            for T in range(2):
                slot = 2 * g + T
                p.dma("ACT", f"out{T}", lambda e, slot=slot, T=T: e.dma_start(
                    out=out_d[128 * slot:128 * (slot + 1), :], in_=ost[T]),
                    reads=[b_ost[T]], writes=[b_out])
        al.pop()
        p.barrier()

        if trunc is not None:
            for q in QUEUES:
                p.steps[q] = [st for st in p.steps[q] if st.seq <= trunc]
        p.finalize()
        ssems = {}
        for nm in p.slots:
            ssems[nm] = es.enter_context(nc.semaphore("z_" + nm))

        @block.sync
        def _(sync):
            p.emit_queue("SYNC", sync, sems, ssems)

        @block.gpsimd
        def _(g):
            p.emit_queue("GPQ", g, sems, ssems)

        @block.tensor
        def _(t):
            p.emit_queue("PE", t, sems, ssems)

        @block.scalar
        def _(sc):
            p.emit_queue("ACT", sc, sems, ssems)

        @block.vector
        def _(v):
            p.emit_queue("DVE", v, sems, ssems)

    return nc


def make_inputs(x, Wk, Wq, Wv, fk, fb, S, E):
    B = x.shape[0]
    wkT = np.ascontiguousarray(Wk.T)
    wqT = np.ascontiguousarray(Wq.T)
    wvT = np.ascontiguousarray(Wv.T)
    fk = np.ascontiguousarray(fk)
    fb = np.ascontiguousarray(fb.reshape(1, E))
    ident = np.eye(128, dtype=np.float32)
    jj = np.arange(512)[None, :]
    pp = np.arange(128)[:, None]
    masks = {}
    for par in (0, 1):
        m = np.empty((2, 128, 512), dtype=np.float32)
        for T in (0, 1):
            v = 2 * par + T
            m[T] = np.where(jj <= pp + 128 * v, 0.0, NEGBIG).astype(np.float32)
        masks[par] = m
    in_maps = []
    for c in range(2 * B):
        b, par = c // 2, c % 2
        xb = np.ascontiguousarray(x[b])
        xown = np.ascontiguousarray(
            xb.reshape(S // 256, 256, E)[par::2].reshape(S // 2, E))
        in_maps.append({
            "emb": xb, "x_own": xown,
            "wkT": wkT, "wqT": wqT, "wvT": wvT,
            "fker": fk, "fbias": fb,
            "masks": masks[par], "ident": ident,
        })
    return in_maps


def assemble_out(results, B, S, E):
    out = np.empty((B, S, E), dtype=np.float32)
    for c in range(2 * B):
        b, par = c // 2, c % 2
        out[b].reshape(S // 256, 256, E)[par::2] = \
            np.asarray(results[c]["out"]).reshape(S // 512, 256, E)
    return out


_NC_CACHE = {}


def kernel(L0_attn_embedded, L0_attn_Wk, L0_attn_Wq, L0_attn_Wv,
           L0_ffn_kernel, L0_ffn_bias):
    S, E = 4096, 2048
    x = np.ascontiguousarray(np.asarray(L0_attn_embedded, dtype=np.float32))
    B = x.shape[0]
    in_maps = make_inputs(
        x,
        np.asarray(L0_attn_Wk, dtype=np.float32),
        np.asarray(L0_attn_Wq, dtype=np.float32),
        np.asarray(L0_attn_Wv, dtype=np.float32),
        np.asarray(L0_ffn_kernel, dtype=np.float32),
        np.asarray(L0_ffn_bias, dtype=np.float32),
        S, E)
    if (S, E) not in _NC_CACHE:
        _NC_CACHE[(S, E)] = build_nc(S, E)
    nc = _NC_CACHE[(S, E)]
    res = run_bass_kernel_spmd(nc, in_maps, core_ids=list(range(2 * B)))
    return assemble_out(res.results, B, S, E)


# version 11 (final)
# speedup vs baseline: 1.0523x; 1.0523x over previous
import math
import numpy as np
import ml_dtypes
from contextlib import ExitStack

import concourse.bass as bass
import concourse.mybir as mybir
from concourse.bass_utils import run_bass_kernel_spmd

F32 = mybir.dt.float32
BF16 = mybir.dt.bfloat16
I32 = mybir.dt.int32
AF = mybir.ActivationFunctionType
ALU = mybir.AluOpType
AX = mybir.AxisListType

POOL_W = 48128
NEGBIG = -1.0e30
AOFF = 65536.0
AOFF_SC = AOFF * 1e-4

QUEUES = ("SYNC", "GPQ", "PE", "ACT", "DVE")
CLASSES = {
    "POOL": ("GPQ", 1),
    "PE": ("PE", 1),
    "ACT": ("ACT", 1),
    "DVE": ("DVE", 1),
}


class Buf:
    __slots__ = ("name", "w", "readers", "slices")

    def __init__(self, name, slices=False):
        self.name = name
        self.w = None
        self.readers = []
        self.slices = slices


class Step:
    __slots__ = ("cls", "fn", "waits", "idx", "force_inc", "slot", "seq")

    def __init__(self, cls, fn, waits, idx, force_inc, slot=None):
        self.cls = cls
        self.fn = fn
        self.waits = waits
        self.idx = idx
        self.force_inc = force_inc
        self.slot = slot
        self.seq = -1


class Slot:
    __slots__ = ("name", "count")

    def __init__(self, name):
        self.name = name
        self.count = 0


class Prog:

    def __init__(self, pool_on_dve=False):
        self.pool_on_dve = pool_on_dve
        self.seq = 0
        self.steps = {q: [] for q in QUEUES}
        self.icount = {c: 0 for c in CLASSES}
        self.waited = {q: {} for q in QUEUES}
        self.used = set()
        self.slots = {}

    def slot(self, name):
        if name not in self.slots:
            self.slots[name] = Slot(name)
        return self.slots[name]

    def _resolve(self, q, want, marker):
        waits = []
        wmap = self.waited[q]
        for m in want:
            if m == marker:
                continue
            if m[0] == "S":
                val = self.slots[m[1]].count
                m = ("S", m[1], val)
                key = ("S", m[1])
            else:
                key, val = m[0], m[1]
            if wmap.get(key, -1) >= val:
                continue
            wmap[key] = val
            if m[0] != "S":
                self.used.add(m)
            waits.append(m)
        return waits

    def _track(self, reads, writes, marker):
        want = []
        for b in reads:
            if b.w is not None:
                want.append(b.w)
        for b in writes:
            if b.w is not None and not b.slices:
                want.append(b.w)
            want.extend(b.readers)
        return want

    def step(self, cls, fn, reads=(), writes=(), force_inc=False):
        if cls == "POOL" and self.pool_on_dve:
            cls = "DVE"
        q = CLASSES[cls][0]
        idx = self.icount[cls]
        self.icount[cls] += 1
        marker = (cls, idx)
        want = self._track(reads, writes, marker)
        waits = self._resolve(q, want, marker)
        st = Step(cls, fn, waits, idx, force_inc)
        st.seq = self.seq
        self.seq += 1
        self.steps[q].append(st)
        for b in writes:
            b.w = marker
            b.readers = []
        for b in reads:
            b.readers.append(marker)
        return marker

    def dma(self, q, slot_name, fn, reads=(), writes=()):
        sl = self.slot(slot_name)
        want = self._track(reads, writes, None)
        waits = self._resolve(q, want, None)
        sl.count += 16
        marker = ("S", sl.name, sl.count)
        st = Step(None, fn, waits, None, False, slot=sl.name)
        st.seq = self.seq
        self.seq += 1
        self.steps[q].append(st)
        for b in writes:
            b.w = marker
            b.readers = []
        for b in reads:
            b.readers.append(marker)
        return marker

    def wait_only(self, q, markers):
        waits = self._resolve(q, markers, None)
        if waits:
            st = Step(None, None, waits, None, False)
            st.seq = self.seq
            self.seq += 1
            self.steps[q].append(st)

    def barrier(self):
        last = [(c, self.icount[c] - 1) for c in CLASSES if self.icount[c] > 0]
        last += [("S", sl.name, sl.count) for sl in self.slots.values() if sl.count > 0]
        for q in QUEUES:
            self.wait_only(q, last)

    def finalize(self):
        for q in QUEUES:
            for st in self.steps[q]:
                if st.force_inc and st.cls is not None:
                    self.used.add((st.cls, st.idx))
        self.rank = {}
        for c in CLASSES:
            idxs = sorted(i for (cc, i) in self.used if cc == c)
            for r, i in enumerate(idxs):
                self.rank[(c, i)] = r + 1

    def emit_queue(self, q, eng, sems, ssems):
        for st in self.steps[q]:
            for m in st.waits:
                if m[0] == "S":
                    eng.wait_ge(ssems[m[1]], m[2])
                else:
                    eng.wait_ge(sems[m[0]], self.rank[m])
            if st.fn is not None:
                inst = st.fn(eng)
                if st.slot is not None:
                    inst.then_inc(ssems[st.slot], 16)
                elif (st.cls, st.idx) in self.used:
                    inst.then_inc(sems[st.cls], CLASSES[st.cls][1])


class Carver:
    def __init__(self, width):
        self.width = width
        self.off = 0
        self.marks = []
        self.peak = 0

    def take(self, n_f32, align=16):
        o = (self.off + align - 1) // align * align
        assert o + n_f32 <= self.width, f"SBUF pool overflow: {o}+{n_f32} > {self.width}"
        self.off = o + n_f32
        self.peak = max(self.peak, self.off)
        return o

    def push(self):
        self.marks.append(self.off)

    def pop(self):
        self.off = self.marks.pop()


def _splits(width, maxw=512):
    o = 0
    out = []
    while o < width:
        w = min(maxw, width - o)
        out.append((o, w))
        o += w
    return out


def build_nc(S, E, trunc=None, pool_on_dve=False):
    NCH = E // 128
    NSTRIP = S // 512
    NB = S // 512
    OWNT = S // 256
    NSO = OWNT // 4
    RSQ = 1.0 / math.sqrt(E)
    invE = 1.0 / E
    EH = E // 2
    PVW = max(EH, 512)

    nc = bass.Bass("TRN2", target_bir_lowering=False)

    emb = nc.dram_tensor("emb", [S, E], F32, kind="ExternalInput").ap()
    x_own = nc.dram_tensor("x_own", [OWNT * 128, E], F32, kind="ExternalInput").ap()
    wkT = nc.dram_tensor("wkT", [E, E], F32, kind="ExternalInput").ap()
    wqT = nc.dram_tensor("wqT", [E, E], F32, kind="ExternalInput").ap()
    wvT = nc.dram_tensor("wvT", [E, E], F32, kind="ExternalInput").ap()
    fker = nc.dram_tensor("fker", [E, E], F32, kind="ExternalInput").ap()
    fbias = nc.dram_tensor("fbias", [1, E], F32, kind="ExternalInput").ap()
    masksI = nc.dram_tensor("masks", [2, 128, 512], F32, kind="ExternalInput").ap()
    identI = nc.dram_tensor("ident", [128, 128], F32, kind="ExternalInput").ap()
    out_d = nc.dram_tensor("out", [OWNT * 128, E], F32, kind="ExternalOutput").ap()

    kS = nc.dram_tensor("kS", [S, E], BF16, kind="Internal").ap()
    vS = nc.dram_tensor("vS", [S, E], BF16, kind="Internal").ap()
    qS = nc.dram_tensor("qS", [OWNT * 128, E], BF16, kind="Internal").ap()
    nxS = nc.dram_tensor("nxS", [OWNT, NCH, 128, 128], BF16, kind="Internal").ap()
    nyS = nc.dram_tensor("nyS", [OWNT * 128, E], BF16, kind="Internal").ap()
    rowS = nc.dram_tensor("rowS", [1, E], F32, kind="Internal").ap()

    p = Prog(pool_on_dve=pool_on_dve)
    es = ExitStack()
    with es:
        pool_h = es.enter_context(nc.sbuf_tensor("pool", [128, POOL_W], F32))
        ps_h = es.enter_context(nc.psum_tensor("pspool", [128, 4096], F32))
        sems = {}
        for c, nm in [("POOL", "sPL"), ("PE", "sPE"), ("ACT", "sAC"), ("DVE", "sDV")]:
            sems[c] = es.enter_context(nc.semaphore(nm))
        block = es.enter_context(nc.Block())

        pool_f = pool_h.ap()
        pool_b = pool_h.bitcast(BF16).ap()
        pool_i = pool_h.bitcast(I32).ap()
        psf = ps_h.ap()
        psb = ps_h.bitcast(BF16).ap()

        def cf(off, w):
            return pool_f[:, off:off + w]

        def cb(off, w):
            return pool_b[:, 2 * off:2 * off + w]

        def cI(off, w):
            return pool_i[:, off:off + w]

        al = Carver(POOL_W)

        psA = psf[:, 0:1024]
        psB = psf[:, 1024:3072]
        psTh = [psb[:, 6144:6656], psb[:, 7168:7680]]
        psN = psf[:, 3072:4096]
        psF = psf[:, 3328:3584]
        bA = [Buf("psA0"), Buf("psA1")]
        bB = [Buf("psB0"), Buf("psB1")]
        bT = [Buf("psT0"), Buf("psT1")]
        bF = [Buf("psF0"), Buf("psF1")]

        o_identf = al.take(128)
        o_identb = al.take(64)
        o_ones = al.take(128)
        o_masks = al.take(1024)
        o_srow = al.take(max(OWNT, 16))
        o_sy = al.take(max(OWNT, 16))
        o_tiny = al.take(64)
        identf = cf(o_identf, 128)
        identb = cb(o_identb, 128)
        ones_f = cf(o_ones, 128)
        mask_sb = cf(o_masks, 1024)
        srowP = cf(o_srow, OWNT)
        syP = cf(o_sy, OWNT)

        def tiny(g, i):
            return cf(o_tiny + 8 * g + i, 1)

        b_identf = Buf("identf")
        b_identb = Buf("identb")
        b_ones = Buf("ones")
        b_masks = Buf("masks")
        b_srow = Buf("srowP")
        b_sy = Buf("syP")
        b_tiny = [Buf(f"tiny{i}") for i in range(8)]
        b_ktS = Buf("ktS", slices=True)
        b_vS = Buf("vS", slices=True)
        b_qtS = Buf("qtS", slices=True)
        b_nxS = Buf("nxS", slices=True)
        b_nyS = Buf("nyS", slices=True)
        b_rowS = Buf("rowS")
        b_out = Buf("out", slices=True)

        p.dma("SYNC", "miscS", lambda e: e.dma_start(out=identf, in_=identI), writes=[b_identf])
        p.dma("GPQ", "identG", lambda e: e.dma_start(out=identb, in_=identI), writes=[b_identb])
        p.dma("SYNC", "miscS", lambda e: e.dma_start(out=mask_sb[:, 0:512], in_=masksI[0]), writes=[b_masks])
        p.dma("SYNC", "miscS", lambda e: e.dma_start(out=mask_sb[:, 512:1024], in_=masksI[1]), writes=[b_masks])
        p.step("DVE", lambda e: e.memset(ones_f[0:1, :], 1.0), writes=[b_ones])

        o_consts = al.take(8)
        b_const = Buf("consts")
        for i, v in enumerate([1e-5, RSQ, 128.0, AOFF, -128.0]):
            ap = cf(o_consts + i, 1)
            p.step("DVE", lambda e, ap=ap, v=v: e.memset(ap, v), writes=[b_const])
            nc.const_aps.register(F32, v, ap)
        p.barrier()

        def ln_tile(xap, b_x, xnb_ap, b_xnb, sq_ap, b_sq, tg):
            btg = b_tiny[tg]
            s1, s2, mm, var, sd, rr = (tiny(tg, i) for i in range(6))
            p.step("ACT", lambda e: e.activation(sq_ap, xap, AF.Copy, accum_out=s1),
                   reads=[b_x], writes=[b_sq, btg])
            p.step("ACT", lambda e: e.activation(sq_ap, xap, AF.Square, accum_out=s2),
                   reads=[b_x], writes=[b_sq, btg])
            p.step("DVE", lambda e: e.tensor_scalar(mm, s1, invE, None, op0=ALU.mult),
                   reads=[btg], writes=[btg])
            p.step("DVE", lambda e: e.tensor_scalar(var, s2, invE, None, op0=ALU.mult),
                   reads=[btg], writes=[btg])
            p.step("DVE", lambda e: e.scalar_tensor_tensor(sd, mm, 1.0, mm, op0=ALU.mult, op1=ALU.mult),
                   reads=[btg], writes=[btg])
            p.step("DVE", lambda e: e.tensor_sub(var, var, sd), reads=[btg], writes=[btg])
            p.step("ACT", lambda e: e.activation(sd, var, AF.Sqrt, bias=1e-5),
                   reads=[btg], writes=[btg])
            p.step("DVE", lambda e: e.reciprocal(rr, sd), reads=[btg], writes=[btg])
            negmr = tiny(tg, 4)
            p.step("DVE", lambda e: e.tensor_scalar(negmr, mm, rr, -1.0, op0=ALU.mult, op1=ALU.mult),
                   reads=[btg], writes=[btg])
            p.step("DVE", lambda e: e.tensor_scalar(xnb_ap, xap, rr, negmr, op0=ALU.mult, op1=ALU.add),
                   reads=[b_x, btg], writes=[b_xnb])
            return rr, negmr, mm

        def transpose_chunks(src_ap, b_src, nchunks, dst_fn, evac_cls="DVE", par=0):
            assert nchunks % 4 == 0
            for j in range(nchunks // 4):
                half = (j + par) % 2
                pst = psTh[half]
                for cc in range(4):
                    c = 4 * j + cc
                    psl = pst[:, 128 * cc:128 * cc + 128]
                    sap = src_ap[:, 128 * c:128 * c + 128]
                    p.step("PE", lambda e, o=psl, i=sap: e.transpose(o, i, identb),
                           reads=[b_src, b_identb], writes=[bT[half]])
                pv = pst[:, 0:512]
                dap, dbuf = dst_fn(j)
                if evac_cls == "ACT":
                    p.step("ACT", lambda e, o=dap, i=pv: e.activation(o, i, AF.Copy),
                           reads=[bT[half]], writes=[dbuf])
                else:
                    p.step(evac_cls, lambda e, o=dap, i=pv: e.tensor_copy(o, i),
                           reads=[bT[half]], writes=[dbuf])

        al.push()
        o_w1 = al.take(NCH * E // 2)
        o_w2 = al.take(NCH * E // 2)
        o_xd = [al.take(E)]
        o_xnb4 = al.take(4 * E // 2)
        o_xnT = al.take(NCH * 512 // 2)
        o_kev = al.take(E // 2)
        o_vev = al.take(E // 2)
        W1 = cb(o_w1, NCH * E)
        W2 = cb(o_w2, NCH * E)
        xd = [cf(o_xd[0], E), cf(o_xd[0], E)]
        xnb4 = cb(o_xnb4, 4 * E)
        xnT = cb(o_xnT, NCH * 512)
        xnT3 = xnT.rearrange("p (c k) -> p c k", k=512)
        kev = cb(o_kev, E)
        vev = cb(o_vev, E)
        b_W1 = Buf("W1", slices=True)
        b_W2 = Buf("W2", slices=True)
        b_xd1 = Buf("xd")
        b_xd = [b_xd1, b_xd1]
        b_xnb4 = [Buf(f"xnb4_{i}") for i in range(4)]
        b_xnTc = [Buf(f"xnTc{t}") for t in range(4)]
        b_kev = Buf("kev")
        b_vev = Buf("vev")

        for c in range(NCH):
            p.dma("GPQ", "wload", lambda e, c=c: e.dma_start(
                out=W1[:, E * c:E * (c + 1)], in_=wkT[128 * c:128 * (c + 1), :]),
                writes=[b_W1])
            p.dma("GPQ", "wload", lambda e, c=c: e.dma_start(
                out=W2[:, E * c:E * (c + 1)], in_=wvT[128 * c:128 * (c + 1), :]),
                writes=[b_W2])

        def p1_ln(strip, t):
            til = 4 * strip + t
            sl = til % 2
            xn_t = xnb4[:, E * t:E * (t + 1)]
            p.dma("SYNC", f"x{sl}", lambda e, til=til, sl=sl: e.dma_start(
                out=xd[sl], in_=emb[128 * til:128 * (til + 1), :]), writes=[b_xd[sl]])
            ln_tile(xd[sl], b_xd[sl], xn_t, b_xnb4[t], xn_t, b_xnb4[t], t)

        tpar = [0]

        def p1_tbatch(strip, t, j):
            half = tpar[0] % 2
            tpar[0] += 1
            pst = psTh[half]
            src = xnb4[:, E * t:E * (t + 1)]
            for cc in range(4):
                c = 4 * j + cc
                psl = pst[:, 128 * cc:128 * cc + 128]
                sap = src[:, 128 * c:128 * c + 128]
                p.step("PE", lambda e, o=psl, i=sap: e.transpose(o, i, identb),
                       reads=[b_xnb4[t], b_identb], writes=[bT[half]])
            dap = xnT3[:, 4 * j:4 * j + 4, 128 * t:128 * t + 128]
            p.step("DVE", lambda e, o=dap, i=pst[:, 0:512]: e.tensor_copy(o, i),
                   reads=[bT[half]], writes=[b_xnTc[t]])

        for t in range(4):
            p1_ln(0, t)
            for j in range(NCH // 4):
                p1_tbatch(0, t, j)

        pend = []
        for s in range(NSTRIP):
            for t in range(4):
                for c in range(NCH):
                    for fi, (fo, fw) in enumerate(_splits(E)):
                        p.step("PE", lambda e, c=c, t=t, fo=fo, fw=fw,
                               st=(c == 0), sp=(c == NCH - 1):
                               e.matmul(psB[:, fo:fo + fw],
                                        xnT[:, 512 * c + 128 * t:512 * c + 128 * t + 128],
                                        W1[:, E * c + fo:E * c + fo + fw], start=st, stop=sp),
                               reads=[b_xnTc[t], b_W1], writes=[bB[(2 * fo) // E]])
                    if c % 4 == 3 and pend:
                        fn = pend.pop(0)
                        fn()
                p.step("ACT", lambda e: e.activation(kev, psB[:, 0:E], AF.Copy),
                       reads=[bB[0], bB[1]], writes=[b_kev])
                p.dma("ACT", "kev", lambda e, til=4 * s + t: e.dma_start(
                    out=kS[128 * til:128 * (til + 1), :], in_=kev),
                    reads=[b_kev], writes=[b_ktS])
                for c in range(NCH):
                    for fi, (fo, fw) in enumerate(_splits(E)):
                        if fo < 1024:
                            pv = psA[:, fo:fo + fw]
                            wb_ = bA[fi % 2]
                        else:
                            pv = psN[:, fo - 1024:fo - 1024 + fw]
                            wb_ = bT[fi % 2]
                        p.step("PE", lambda e, pv=pv, c=c, t=t, fo=fo, fw=fw,
                               st=(c == 0), sp=(c == NCH - 1):
                               e.matmul(pv, xnT[:, 512 * c + 128 * t:512 * c + 128 * t + 128],
                                        W2[:, E * c + fo:E * c + fo + fw], start=st, stop=sp),
                               reads=[b_xnTc[t], b_W2], writes=[wb_])
                p.step("DVE", lambda e: e.tensor_copy(vev[:, 0:min(E, 1024)], psA[:, 0:min(E, 1024)]),
                       reads=[bA[0], bA[1]], writes=[b_vev])
                if E > 1024:
                    p.step("DVE", lambda e: e.tensor_copy(vev[:, 1024:E], psN[:, 0:E - 1024]),
                           reads=[bT[0], bT[1]], writes=[b_vev])
                p.dma("ACT", "vev", lambda e, til=4 * s + t: e.dma_start(
                    out=vS[128 * til:128 * (til + 1), :], in_=vev),
                    reads=[b_vev], writes=[b_vS])
                if s + 1 < NSTRIP:
                    p1_ln(s + 1, t)
                    for j in range(NCH // 4):
                        pend.append(lambda strip=s + 1, t=t, j=j: p1_tbatch(strip, t, j))
        for fn in pend:
            fn()
        pend = []
        al.pop()
        p.barrier()

        al.push()
        o_wq = al.take(NCH * E // 2)
        o_xd2 = [al.take(E), al.take(E)]
        o_xnb4 = al.take(4 * E // 2)
        o_nxb4 = al.take(4 * E // 2)
        o_xnT2 = al.take(NCH * 512 // 2)
        o_qev = al.take(E // 2)
        o_ci2 = al.take(E)
        o_f2 = al.take(E)
        o_r12 = al.take(E)
        o_nxT = al.take(E)
        WQ = cb(o_wq, NCH * E)
        xd2 = [cf(o_xd2[0], E), cf(o_xd2[1], E)]
        xnb4 = cb(o_xnb4, 4 * E)
        nxb4 = cb(o_nxb4, 4 * E)
        xnT2 = cb(o_xnT2, NCH * 512)
        xnT23 = xnT2.rearrange("p (c k) -> p c k", k=512)
        qev = cb(o_qev, E)
        ci2 = cI(o_ci2, E)
        f2 = cf(o_f2, E)
        r12 = cf(o_r12, E)
        nxT = [cb(o_nxT, E), cb(o_nxT + E // 2, E)]
        b_WQ = Buf("WQ", slices=True)
        b_xd2 = [Buf("x2a"), Buf("x2b")]
        b_xnb24 = [Buf(f"xnb24_{i}") for i in range(4)]
        b_nxb4 = [Buf(f"nxb4_{i}") for i in range(4)]
        b_xnT2c = [Buf(f"xnT2c{t}") for t in range(4)]
        b_qev = Buf("qev")
        b_ci2 = Buf("ci2")
        b_f2 = Buf("f2")
        b_r12 = Buf("r12")
        b_nxT = [Buf("nxT0"), Buf("nxT1")]

        for c in range(NCH):
            p.dma("GPQ", "wload", lambda e, c=c: e.dma_start(
                out=WQ[:, E * c:E * (c + 1)], in_=wqT[128 * c:128 * (c + 1), :]),
                writes=[b_WQ])

        def p2_ln(so, t):
            slot = 4 * so + t
            sl = slot % 2
            xn_t = xnb4[:, E * t:E * (t + 1)]
            nx_t = nxb4[:, E * t:E * (t + 1)]
            p.dma("SYNC", f"x{sl}", lambda e, slot=slot, sl=sl: e.dma_start(
                out=xd2[sl], in_=x_own[128 * slot:128 * (slot + 1), :]),
                writes=[b_xd2[sl]])
            rr, negmr, mm = ln_tile(xd2[sl], b_xd2[sl], xn_t, b_xnb24[t], xn_t, b_xnb24[t], t)
            qg = 4 + t
            bqg = b_tiny[qg]
            am, fx, am2, r127, rrt, biasq, thr = (tiny(qg, i) for i in range(7))
            p.step("DVE", lambda e, am=am, xn_t=xn_t, t=t: e.reduce_max(
                am, xn_t, axis=AX.X, apply_absolute_value=True),
                reads=[b_xnb24[t]], writes=[bqg])
            p.step("ACT", lambda e, fx=fx, am=am: e.activation(fx, am, AF.Sign), reads=[bqg], writes=[bqg])
            p.step("DVE", lambda e, fx=fx: e.tensor_scalar(fx, fx, -1.0, 1.0, op0=ALU.mult, op1=ALU.add),
                   reads=[bqg], writes=[bqg])
            p.step("DVE", lambda e, am2=am2, am=am, fx=fx: e.tensor_add(am2, am, fx), reads=[bqg], writes=[bqg])
            p.step("DVE", lambda e, r127=r127, am2=am2: e.reciprocal(r127, am2), reads=[bqg], writes=[bqg])
            p.step("DVE", lambda e, r127=r127: e.tensor_scalar(r127, r127, 127.0, None, op0=ALU.mult),
                   reads=[bqg], writes=[bqg])
            p.step("DVE", lambda e, slot=slot, am2=am2: e.tensor_scalar(
                srowP[:, slot:slot + 1], am2, 1.0 / 127.0, None, op0=ALU.mult),
                reads=[bqg], writes=[b_srow])
            p.step("DVE", lambda e, rr=rr, rrt=rrt, r127=r127: e.tensor_tensor(rrt, rr, r127, op=ALU.mult),
                   reads=[b_tiny[t], bqg], writes=[bqg])
            p.step("DVE", lambda e, mm=mm, biasq=biasq, rrt=rrt: e.tensor_scalar(
                biasq, mm, rrt, -1.0, op0=ALU.mult, op1=ALU.mult),
                reads=[b_tiny[t], bqg], writes=[bqg])
            p.step("DVE", lambda e, biasq=biasq: e.tensor_scalar(biasq, biasq, 128.0, None, op0=ALU.add),
                   reads=[bqg], writes=[bqg])
            p.step("DVE", lambda e, thr=thr, biasq=biasq: e.tensor_scalar(
                thr, biasq, -1.0, 0.5, op0=ALU.mult, op1=ALU.add),
                reads=[bqg], writes=[bqg])
            p.step("ACT", lambda e, x=xd2[sl], rrt=rrt, biasq=biasq: e.activation(
                ci2, x, AF.Identity, scale=rrt, bias=biasq),
                reads=[b_xd2[sl], bqg], writes=[b_ci2])
            p.step("DVE", lambda e, x=xd2[sl], rrt=rrt: e.scalar_tensor_tensor(
                f2, x, rrt, ci2, op0=ALU.mult, op1=ALU.subtract),
                reads=[b_xd2[sl], bqg, b_ci2], writes=[b_f2])
            p.step("DVE", lambda e, thr=thr: e.scalar_tensor_tensor(
                r12, f2, thr, ci2, op0=ALU.is_gt, op1=ALU.add),
                reads=[b_f2, bqg, b_ci2], writes=[b_r12])
            p.step("ACT", lambda e, nx_t=nx_t, t=t: e.activation(nx_t, r12, AF.Copy, bias=-128.0),
                   reads=[b_r12], writes=[b_nxb4[t]])

        def p2_tbatch(so, t, j):
            half = tpar[0] % 2
            tpar[0] += 1
            pst = psTh[half]
            src = xnb4[:, E * t:E * (t + 1)]
            for cc in range(4):
                c = 4 * j + cc
                psl = pst[:, 128 * cc:128 * cc + 128]
                sap = src[:, 128 * c:128 * c + 128]
                p.step("PE", lambda e, o=psl, i=sap: e.transpose(o, i, identb),
                       reads=[b_xnb24[t], b_identb], writes=[bT[half]])
            dap = xnT23[:, 4 * j:4 * j + 4, 128 * t:128 * t + 128]
            p.step("DVE", lambda e, o=dap, i=pst[:, 0:512]: e.tensor_copy(o, i),
                   reads=[bT[half]], writes=[b_xnT2c[t]])

        def p2_nxbatch(so, t, j):
            half = tpar[0] % 2
            tpar[0] += 1
            pst = psTh[half]
            nsl = t % 2
            src = nxb4[:, E * t:E * (t + 1)]
            for cc in range(4):
                c = 4 * j + cc
                psl = pst[:, 128 * cc:128 * cc + 128]
                sap = src[:, 128 * c:128 * c + 128]
                p.step("PE", lambda e, o=psl, i=sap: e.transpose(o, i, identb),
                       reads=[b_nxb4[t], b_identb], writes=[bT[half]])
            p.step("ACT", lambda e, o=nxT[nsl][:, 512 * j:512 * j + 512], i=pst[:, 0:512]:
                   e.activation(o, i, AF.Copy),
                   reads=[bT[half]], writes=[b_nxT[nsl]])
            if j == NCH // 4 - 1:
                nxT3 = nxT[nsl].rearrange("p (c j) -> p c j", j=128)
                slot = 4 * so + t
                p.dma("ACT", "nxSp", lambda e, slot=slot, nxT3=nxT3: e.dma_start(
                    out=nxS[slot].rearrange("c p j -> p c j"), in_=nxT3),
                    reads=[b_nxT[nsl]], writes=[b_nxS])

        for t in range(4):
            p2_ln(0, t)
            for j in range(NCH // 4):
                p2_tbatch(0, t, j)
            for j in range(NCH // 4):
                p2_nxbatch(0, t, j)

        pend2 = []
        for so in range(NSO):
            for t in range(4):
                slot = 4 * so + t
                for c in range(NCH):
                    for fi, (fo, fw) in enumerate(_splits(E)):
                        p.step("PE", lambda e, c=c, t=t, fo=fo, fw=fw,
                               st=(c == 0), sp=(c == NCH - 1):
                               e.matmul(psB[:, fo:fo + fw],
                                        xnT2[:, 512 * c + 128 * t:512 * c + 128 * t + 128],
                                        WQ[:, E * c + fo:E * c + fo + fw], start=st, stop=sp),
                               reads=[b_xnT2c[t], b_WQ], writes=[bB[(2 * fo) // E]])
                    if c % 2 == 1 and pend2:
                        fn = pend2.pop(0)
                        fn()
                p.step("ACT", lambda e: e.activation(qev, psB[:, 0:E], AF.Copy, scale=RSQ),
                       reads=[bB[0], bB[1]], writes=[b_qev])
                p.dma("ACT", "qev", lambda e, slot=slot: e.dma_start(
                    out=qS[128 * slot:128 * (slot + 1), :], in_=qev),
                    reads=[b_qev], writes=[b_qtS])
                if so + 1 < NSO:
                    p2_ln(so + 1, t)
                    for j in range(NCH // 4):
                        pend2.append(lambda so2=so + 1, t=t, j=j: p2_tbatch(so2, t, j))
                    for j in range(NCH // 4):
                        pend2.append(lambda so2=so + 1, t=t, j=j: p2_nxbatch(so2, t, j))
        for fn in pend2:
            fn()
        pend2 = []
        al.pop()
        p.barrier()

        al.push()
        o_nk = al.take(NCH * E // 2)
        o_bqb = al.take(E)
        NK = cb(o_nk, NCH * E)
        bqb = cf(o_bqb, E)
        b_NK = [Buf(f"NK_{i}") for i in range(NCH)]
        b_bqb = Buf("bqb")

        al.push()
        o_fk = [al.take(E), al.take(E)]
        o_tab = al.take(E)
        o_rcb = al.take(E)
        o_scb = al.take(E)
        o_scp = al.take(NCH)
        o_ar = al.take(128)
        o_row0 = al.take(E)
        o_row1 = al.take(E)
        o_row2 = al.take(E)
        o_ci3 = al.take(E)
        o_f3 = al.take(E)
        o_r13 = al.take(E)
        o_cirow = o_ci3
        o_M = o_f3
        fk = [cf(o_fk[0], E), cf(o_fk[1], E)]
        Mt = cf(o_M, E)
        tab = cf(o_tab, E)
        rcb = cf(o_rcb, E)
        scb = cf(o_scb, E)
        scp = cf(o_scp, NCH)
        arow = cf(o_ar, 128)
        row0 = cf(o_row0, E)
        row1 = cf(o_row1, E)
        row2 = cf(o_row2, E)
        cirow = cI(o_cirow, E)
        ci3 = cI(o_ci3, E)
        f3 = cf(o_f3, E)
        r13 = cf(o_r13, E)
        b_fk = [Buf("fk0"), Buf("fk1")]
        b_tab = Buf("tab")
        b_rcb = Buf("rcb")
        b_scb = Buf("scb")
        b_scp = Buf("scp")
        b_ar = Buf("arow")
        b_row = [Buf("row0"), Buf("row1"), Buf("row2")]
        b_ci3 = Buf("ci3")
        b_cirow = b_ci3
        b_f3 = Buf("f3")
        b_M = b_f3
        b_r13 = Buf("r13")

        for c in range(NCH):
            sl = c % 2
            p.dma("SYNC", f"fk{sl}", lambda e, c=c, sl=sl: e.dma_start(
                out=fk[sl], in_=fker[128 * c:128 * (c + 1), :]), writes=[b_fk[sl]])
            p.step("ACT", lambda e, sl=sl: e.activation(tab, fk[sl], AF.Abs),
                   reads=[b_fk[sl]], writes=[b_tab])
            if c == 0:
                p.step("DVE", lambda e: e.tensor_copy(Mt, tab), reads=[b_tab], writes=[b_M])
            else:
                p.step("DVE", lambda e: e.tensor_max(Mt, Mt, tab), reads=[b_tab, b_M], writes=[b_M])
        for t in range(NCH):
            pslot = psF[:, 0:128]
            p.step("PE", lambda e, o=pslot, t=t: e.transpose(o, Mt[:, 128 * t:128 * t + 128], identf),
                   reads=[b_M, b_identf], writes=[bF[0]])
            p.step("DVE", lambda e, i=pslot, t=t: e.reduce_max(scp[:, t:t + 1], i, axis=AX.X),
                   reads=[bF[0]], writes=[b_scp])
        p.step("PE", lambda e: e.transpose(psF[0:NCH, 0:128], scp, identf),
               reads=[b_scp, b_identf], writes=[bF[0]])
        p.step("DVE", lambda e: e.tensor_copy(arow[0:NCH, :], psF[0:NCH, 0:128]),
               reads=[bF[0]], writes=[b_ar])
        p.dma("GPQ", "rowW", lambda e: e.dma_start(
            out=rowS.rearrange("o (a b) -> (o a) b", a=NCH, b=128), in_=arow[0:NCH, :]),
            reads=[b_ar], writes=[b_rowS])
        p.dma("GPQ", "rowW", lambda e: e.dma_start(out=row0[0:1, :], in_=rowS),
               reads=[b_rowS], writes=[b_row[0]])
        p.step("ACT", lambda e: e.sign(row1[0:1, :], row0[0:1, :]),
               reads=[b_row[0]], writes=[b_row[1]])
        p.step("DVE", lambda e: e.tensor_scalar(row1[0:1, :], row1[0:1, :], -1.0, 1.0,
                                                op0=ALU.mult, op1=ALU.add),
               reads=[b_row[1]], writes=[b_row[1]])
        p.step("DVE", lambda e: e.tensor_add(row0[0:1, :], row0[0:1, :], row1[0:1, :]),
               reads=[b_row[0], b_row[1]], writes=[b_row[0]])
        p.step("DVE", lambda e: e.reciprocal(row2[0:1, :], row0[0:1, :]),
               reads=[b_row[0]], writes=[b_row[2]])
        p.step("DVE", lambda e: e.tensor_scalar(row2[0:1, :], row2[0:1, :], 127.0, None, op0=ALU.mult),
               reads=[b_row[2]], writes=[b_row[2]])
        p.step("DVE", lambda e: e.tensor_scalar(row1[0:1, :], row0[0:1, :], 1.0 / 127.0, None,
                                                op0=ALU.mult),
               reads=[b_row[0]], writes=[b_row[1]])
        for (srcap, bsrc, dst, bdst) in ((row2, b_row[2], rcb, b_rcb), (row1, b_row[1], scb, b_scb)):
            for (fo, fw) in _splits(E):
                pa = psA[:, 0:fw]
                p.step("PE", lambda e, pa=pa, srcap=srcap, fo=fo, fw=fw: e.matmul(
                    pa, ones_f[0:1, 0:128], srcap[0:1, fo:fo + fw], start=True, stop=True),
                    reads=[bsrc, b_ones], writes=[bA[0]])
                p.step("DVE", lambda e, pa=pa, dst=dst, fo=fo, fw=fw: e.tensor_copy(
                    dst[:, fo:fo + fw], pa), reads=[bA[0]], writes=[bdst])
        p.dma("SYNC", "miscS", lambda e: e.dma_start(out=row0[0:1, :], in_=fbias),
               reads=[b_row[0]], writes=[b_row[0]])
        tb0, tb1, tb2 = tiny(7, 0), tiny(7, 1), tiny(7, 2)
        btb = b_tiny[7]
        p.step("DVE", lambda e: e.reduce_max(tb0[0:1, :], row0[0:1, :], axis=AX.X,
                                             apply_absolute_value=True),
               reads=[b_row[0]], writes=[btb])
        p.step("ACT", lambda e: e.sign(tb1[0:1, :], tb0[0:1, :]), reads=[btb], writes=[btb])
        p.step("DVE", lambda e: e.tensor_scalar(tb1[0:1, :], tb1[0:1, :], -1.0, 1.0,
                                                op0=ALU.mult, op1=ALU.add),
               reads=[btb], writes=[btb])
        p.step("DVE", lambda e: e.tensor_add(tb0[0:1, :], tb0[0:1, :], tb1[0:1, :]),
               reads=[btb], writes=[btb])
        p.step("DVE", lambda e: e.reciprocal(tb1[0:1, :], tb0[0:1, :]), reads=[btb], writes=[btb])
        p.step("DVE", lambda e: e.tensor_scalar(tb1[0:1, :], tb1[0:1, :], 127.0, None, op0=ALU.mult),
               reads=[btb], writes=[btb])
        p.step("DVE", lambda e: e.tensor_scalar(tb2[0:1, :], tb0[0:1, :], 1.0 / 127.0, None,
                                                op0=ALU.mult),
               reads=[btb], writes=[btb])
        p.step("ACT", lambda e: e.activation(cirow[0:1, :], row0[0:1, :], AF.Copy,
                                             scale=tb1[0:1, :], bias=128.0),
               reads=[b_row[0], btb], writes=[b_cirow])
        p.step("DVE", lambda e: e.scalar_tensor_tensor(
            row2[0:1, :], row0[0:1, :], tb1[0:1, :], cirow[0:1, :],
            op0=ALU.mult, op1=ALU.subtract),
            reads=[b_row[0], btb, b_cirow], writes=[b_row[2]])
        p.step("DVE", lambda e: e.scalar_tensor_tensor(
            row1[0:1, :], row2[0:1, :], -127.5, cirow[0:1, :], op0=ALU.is_gt, op1=ALU.add),
            reads=[b_row[2], b_cirow], writes=[b_row[1]])
        p.step("DVE", lambda e: e.tensor_scalar(row1[0:1, :], row1[0:1, :], 128.0, tb2[0:1, :],
                                                op0=ALU.subtract, op1=ALU.mult),
               reads=[b_row[1], btb], writes=[b_row[1]])
        for (fo, fw) in _splits(E):
            pa = psA[:, 0:fw]
            p.step("PE", lambda e, pa=pa, fo=fo, fw=fw: e.matmul(
                pa, ones_f[0:1, 0:128], row1[0:1, fo:fo + fw], start=True, stop=True),
                reads=[b_row[1], b_ones], writes=[bA[0]])
            p.step("DVE", lambda e, pa=pa, fo=fo, fw=fw: e.tensor_copy(
                bqb[:, fo:fo + fw], pa), reads=[bA[0]], writes=[b_bqb])
        for c in range(NCH):
            sl = c % 2
            p.dma("SYNC", f"fk{sl}", lambda e, c=c, sl=sl: e.dma_start(
                out=fk[sl], in_=fker[128 * c:128 * (c + 1), :]),
                reads=[b_fk[sl]], writes=[b_fk[sl]])
            p.step("POOL", lambda e, sl=sl: e.tensor_mul(tab, fk[sl], rcb),
                   reads=[b_fk[sl], b_rcb], writes=[b_tab])
            p.step("ACT", lambda e: e.activation(ci3, tab, AF.Copy, bias=128.0),
                   reads=[b_tab], writes=[b_ci3])
            p.step("DVE", lambda e: e.scalar_tensor_tensor(
                f3, tab, 128.0, ci3, op0=ALU.add, op1=ALU.subtract),
                reads=[b_tab, b_ci3], writes=[b_f3])
            p.step("DVE", lambda e: e.scalar_tensor_tensor(
                r13, f3, 0.5, ci3, op0=ALU.is_gt, op1=ALU.add),
                reads=[b_f3, b_ci3], writes=[b_r13])
            p.step("DVE", lambda e, c=c: e.scalar_tensor_tensor(
                NK[:, E * c:E * (c + 1)], r13, 128.0, scb, op0=ALU.subtract, op1=ALU.mult),
                reads=[b_r13, b_scb], writes=[b_NK[c]])
        al.pop()
        p.barrier()

        al.push()
        o_nxt = [al.take(E // 2), al.take(E // 2)]
        o_y2 = al.take(E)
        o_y3 = al.take(E)
        o_ciy = al.take(E)
        o_fy = al.take(E)
        o_r1y = al.take(E)
        o_nyb = al.take(E // 2)
        nxt = [cb(o_nxt[0], E), cb(o_nxt[1], E)]
        y2 = cf(o_y2, E)
        y3 = cf(o_y3, E)
        ciy = cI(o_ciy, E)
        fy = cf(o_fy, E)
        r1y = cf(o_r1y, E)
        nyb = cb(o_nyb, E)
        b_nxt = [Buf("nxt0", slices=True), Buf("nxt1", slices=True)]
        b_y2 = Buf("y2")
        b_y3 = Buf("y3")
        b_ciy = Buf("ciy")
        b_fy = Buf("fy")
        b_r1y = Buf("r1y")
        b_nyb = Buf("nyb")

        for slot in range(OWNT):
            sl = slot % 2
            p.dma("SYNC", f"nxt{sl}", lambda e, slot=slot, sl=sl: e.dma_start(
                out=nxt[sl].rearrange("p (c j) -> p c j", j=128),
                in_=nxS[slot].rearrange("c p j -> p c j")),
                reads=[b_nxS], writes=[b_nxt[sl]])
            odd = slot % 2
            for c in range(NCH):
                for fi, (fo, fw) in enumerate(_splits(E)):
                    if odd:
                        if fo < 1024:
                            pv = psA[:, fo:fo + fw]
                            wb_ = bA[fi % 2]
                        else:
                            pv = psN[:, fo - 1024:fo - 1024 + fw]
                            wb_ = bT[fi % 2]
                    else:
                        pv = psB[:, fo:fo + fw]
                        wb_ = bB[(2 * fo) // E]
                    p.step("PE", lambda e, pv=pv, c=c, sl=sl, fo=fo, fw=fw,
                           st=(c == 0), sp=(c == NCH - 1): e.matmul(
                        pv, nxt[sl][:, 128 * c:128 * c + 128],
                        NK[:, E * c + fo:E * c + fo + fw], start=st, stop=sp),
                        reads=[b_nxt[sl], b_NK[c]], writes=[wb_])
            fg = 6 + sl
            bfg = b_tiny[fg]
            am, am2, ry = tiny(fg, 0), tiny(fg, 1), tiny(fg, 2)
            if odd:
                p.step("DVE", lambda e, slot=slot: e.scalar_tensor_tensor(
                    y2[:, 0:min(E, 1024)], psA[:, 0:min(E, 1024)], srowP[:, slot:slot + 1],
                    bqb[:, 0:min(E, 1024)], op0=ALU.mult, op1=ALU.add),
                    reads=[bA[0], bA[1], b_srow, b_bqb], writes=[b_y2])
                if E > 1024:
                    p.step("DVE", lambda e, slot=slot: e.scalar_tensor_tensor(
                        y2[:, 1024:E], psN[:, 0:E - 1024], srowP[:, slot:slot + 1],
                        bqb[:, 1024:E], op0=ALU.mult, op1=ALU.add),
                        reads=[bT[0], bT[1], b_srow, b_bqb], writes=[b_y2])
            else:
                p.step("DVE", lambda e, slot=slot: e.scalar_tensor_tensor(
                    y2, psB[:, 0:E], srowP[:, slot:slot + 1], bqb, op0=ALU.mult, op1=ALU.add),
                    reads=[bB[0], bB[1], b_srow, b_bqb], writes=[b_y2])
            p.step("DVE", lambda e, am=am: e.reduce_max(am, y2, axis=AX.X), reads=[b_y2], writes=[bfg])
            p.step("DVE", lambda e, am=am: e.tensor_scalar(am, am, 0.0, None, op0=ALU.max),
                   reads=[bfg], writes=[bfg])
            p.step("ACT", lambda e, am2=am2, am=am: e.sign(am2, am), reads=[bfg], writes=[bfg])
            p.step("DVE", lambda e, am2=am2: e.tensor_scalar(am2, am2, -1.0, 1.0, op0=ALU.mult, op1=ALU.add),
                   reads=[bfg], writes=[bfg])
            p.step("DVE", lambda e, am2=am2, am=am: e.tensor_add(am2, am, am2), reads=[bfg], writes=[bfg])
            p.step("DVE", lambda e, ry=ry, am2=am2: e.reciprocal(ry, am2), reads=[bfg], writes=[bfg])
            p.step("DVE", lambda e, ry=ry: e.tensor_scalar(ry, ry, 127.0, None, op0=ALU.mult),
                   reads=[bfg], writes=[bfg])
            p.step("DVE", lambda e, slot=slot, am2=am2: e.tensor_scalar(
                syP[:, slot:slot + 1], am2, 1.0 / 127.0, None, op0=ALU.mult),
                reads=[bfg], writes=[b_sy])
            p.step("ACT", lambda e, ry=ry: e.activation(y3, y2, AF.Relu, scale=ry),
                   reads=[b_y2, bfg], writes=[b_y3])
            p.step("ACT", lambda e: e.activation(ciy, y3, AF.Copy), reads=[b_y3], writes=[b_ciy])
            p.step("DVE", lambda e: e.tensor_sub(fy, y3, ciy), reads=[b_y3, b_ciy], writes=[b_fy])
            p.step("DVE", lambda e: e.scalar_tensor_tensor(
                r1y, fy, 0.5, ciy, op0=ALU.is_gt, op1=ALU.add),
                reads=[b_fy, b_ciy], writes=[b_r1y])
            p.step("DVE", lambda e: e.tensor_copy(nyb, r1y), reads=[b_r1y], writes=[b_nyb])
            p.dma("ACT", "nySp", lambda e, slot=slot: e.dma_start(
                out=nyS[128 * slot:128 * (slot + 1), :], in_=nyb),
                reads=[b_nyb], writes=[b_nyS])
        al.pop()
        al.pop()
        p.barrier()

        al.push()
        o_SR = [al.take(S), al.take(S)]
        o_Pb = al.take(S // 2)
        o_PT = [al.take(S // 2), al.take(S // 2)]
        o_ktb = [al.take(NCH * 512 // 2), al.take(NCH * 512 // 2)]
        o_vt = [al.take(EH // 2), al.take(EH // 2)]
        o_qt = [[al.take(NCH * 128 // 2), al.take(NCH * 128 // 2)],
                [al.take(NCH * 128 // 2), al.take(NCH * 128 // 2)]]
        o_lp = al.take(2 * max(NB, 8))
        o_x4 = [al.take(E), al.take(E)]
        o_ny4 = [al.take(E // 2), al.take(E // 2)]
        o_ost = [al.take(E), al.take(E)]
        o_ci4 = al.take(EH)
        o_f4 = al.take(EH)
        o_r14 = al.take(EH)
        SR = [cf(o_SR[0], S), cf(o_SR[1], S)]
        Pb = cb(o_Pb, S)
        PT = [cb(o_PT[0], S), cb(o_PT[1], S)]
        ktb = [cb(o_ktb[0], NCH * 512), cb(o_ktb[1], NCH * 512)]
        vt = [cb(o_vt[0], EH), cb(o_vt[1], EH)]
        qt = [[cb(o_qt[0][0], NCH * 128), cb(o_qt[0][1], NCH * 128)],
              [cb(o_qt[1][0], NCH * 128), cb(o_qt[1][1], NCH * 128)]]
        lp = [cf(o_lp, NB), cf(o_lp + NB, NB)]
        x4 = [cf(o_x4[0], E), cf(o_x4[1], E)]
        ny4 = [cb(o_ny4[0], E), cb(o_ny4[1], E)]
        ost = [cf(o_ost[0], E), cf(o_ost[1], E)]
        ci4 = cI(o_ci4, EH)
        f4 = cf(o_f4, EH)
        r14 = cf(o_r14, EH)
        b_SR = [[Buf(f"SR0_{i}") for i in range(NB)], [Buf(f"SR1_{i}") for i in range(NB)]]
        b_Pb = [Buf(f"Pb{i}") for i in range(S // 512)]
        b_PT = [[Buf(f"PT0_{i}") for i in range(S // 512)],
                [Buf(f"PT1_{i}") for i in range(S // 512)]]
        b_ktb = [Buf("ktb0", slices=True), Buf("ktb1", slices=True)]
        b_vt = [Buf("vt0"), Buf("vt1")]
        b_qt = [[Buf("qt00"), Buf("qt01")], [Buf("qt10"), Buf("qt11")]]
        b_lp = [Buf("lp0"), Buf("lp1")]
        b_x4 = [Buf("x40"), Buf("x41")]
        b_ny4 = [Buf("ny40"), Buf("ny41")]
        b_ost = [Buf("ost0"), Buf("ost1")]
        b_ci4 = Buf("ci4")
        b_f4 = Buf("f4")
        b_r14 = Buf("r14")

        def load_qt(g):
            par = g % 2
            for T in range(2):
                slot = 2 * g + T
                for c in range(NCH):
                    p.dma("SYNC", f"qt{par}{T}", lambda e, slot=slot, c=c, par=par, T=T: e.dma_start(
                        out=qt[par][T][:, 128 * c:128 * c + 128],
                        in_=qS[128 * slot:128 * (slot + 1), 128 * c:128 * (c + 1)],
                        transpose=True),
                        reads=[b_qtS], writes=[b_qt[par][T]])

        load_qt(0)
        for g in range(NB):
            KP = 512 * (g + 1)
            nkc = KP // 128
            par = g % 2
            if g + 1 < NB:
                load_qt(g + 1)
            for T in range(2):
                slot = 2 * g + T
                p.dma("SYNC", f"x4{T}", lambda e, slot=slot, T=T: e.dma_start(
                    out=x4[T], in_=x_own[128 * slot:128 * (slot + 1), :]), writes=[b_x4[T]])
                p.dma("SYNC", f"ny4{T}", lambda e, slot=slot, T=T: e.dma_start(
                    out=ny4[T], in_=nyS[128 * slot:128 * (slot + 1), :]),
                    reads=[b_nyS], writes=[b_ny4[T]])
            def load_ktb(s):
                ks = s % 2
                for c in range(NCH):
                    p.dma("SYNC", f"ktb{ks}", lambda e, s=s, ks=ks, c=c: e.dma_start(
                        out=ktb[ks][:, 512 * c:512 * (c + 1)],
                        in_=kS[512 * s:512 * (s + 1), 128 * c:128 * (c + 1)],
                        transpose=True),
                        reads=[b_ktS], writes=[b_ktb[ks]])

            for s in range(g + 1):
                ks = s % 2
                if s > 0 or g == 0:
                    load_ktb(s)
                for T in range(2):
                    sg = 4 + T
                    bsg = b_tiny[sg]
                    rm, tmpm = tiny(sg, 0), tiny(sg, 3)
                    pa = psA[:, 512 * T:512 * T + 512]
                    for c in range(NCH):
                        p.step("PE", lambda e, pa=pa, T=T, c=c, ks=ks, par=par,
                               st=(c == 0), sp=(c == NCH - 1): e.matmul(
                            pa, qt[par][T][:, 128 * c:128 * c + 128],
                            ktb[ks][:, 512 * c:512 * (c + 1)], start=st, stop=sp),
                            reads=[b_qt[par][T], b_ktb[ks]], writes=[bA[T]])
                    if s == g:
                        p.step("DVE", lambda e, T=T, s=s, pa=pa: e.tensor_tensor(
                            SR[T][:, 512 * s:512 * (s + 1)], pa,
                            mask_sb[:, 512 * T:512 * (T + 1)], op=ALU.add),
                            reads=[bA[T], b_masks], writes=[b_SR[T][s]])
                    else:
                        p.step("ACT", lambda e, T=T, s=s, pa=pa: e.activation(
                            SR[T][:, 512 * s:512 * (s + 1)], pa, AF.Copy),
                            reads=[bA[T]], writes=[b_SR[T][s]])
                    dstm = rm if s == 0 else tmpm
                    p.step("DVE", lambda e, T=T, s=s, dstm=dstm: e.reduce_max(
                        dstm, SR[T][:, 512 * s:512 * (s + 1)], axis=AX.X),
                        reads=[b_SR[T][s]], writes=[bsg])
                    if s > 0:
                        p.step("DVE", lambda e, rm=rm, tmpm=tmpm: e.tensor_max(rm, rm, tmpm),
                               reads=[bsg], writes=[bsg])
            if g + 1 < NB:
                load_ktb(0)
            for T in range(2):
                slot = 2 * g + T
                p.step("DVE", lambda e, T=T, slot=slot: e.tensor_scalar(
                    ost[T], ny4[T], syP[:, slot:slot + 1], -AOFF_SC, op0=ALU.mult, op1=ALU.add),
                    reads=[b_ny4[T], b_sy], writes=[b_ost[T]])
                p.step("POOL", lambda e, T=T: e.tensor_add(ost[T], ost[T], x4[T]),
                       reads=[b_ost[T], b_x4[T]], writes=[b_ost[T]])
            for T in range(2):
                sg = 4 + T
                bsg = b_tiny[sg]
                rm, negm = tiny(sg, 0), tiny(sg, 1)
                p.step("DVE", lambda e, rm=rm, negm=negm: e.tensor_scalar(
                    negm, rm, -1.0, None, op0=ALU.mult), reads=[bsg], writes=[bsg])
            for j in range(g + 1):
                for T in range(2):
                    sg = 4 + T
                    bsg = b_tiny[sg]
                    negm = tiny(sg, 1)
                    p.step("ACT", lambda e, T=T, j=j, negm=negm: e.activation(
                        Pb[:, 512 * j:512 * (j + 1)], SR[T][:, 512 * j:512 * (j + 1)],
                        AF.Exp, bias=negm, accum_out=lp[T][:, j:j + 1]),
                        reads=[b_SR[T][j], bsg], writes=[b_Pb[j], b_lp[T]])
                    transpose_chunks(Pb[:, 512 * j:512 * (j + 1)], b_Pb[j], 4,
                                     lambda jj, T=T, j=j: (PT[T][:, 512 * j:512 * j + 512],
                                                           b_PT[T][j]), par=2 * j + T)
                for kc in range(4 * j, 4 * j + 4):
                    vs_ = kc % 2
                    p.dma("SYNC", f"vt{vs_}", lambda e, kc=kc, vs_=vs_: e.dma_start(
                        out=vt[vs_], in_=vS[128 * kc:128 * (kc + 1), 0:EH]),
                        reads=[b_vS], writes=[b_vt[vs_]])
                    for T in range(2):
                        for (fo, fw) in _splits(EH):
                            pv = psB[:, PVW * T + fo:PVW * T + fo + fw]
                            p.step("PE", lambda e, pv=pv, T=T, kc=kc, vs_=vs_, fo=fo, fw=fw,
                                   st=(kc == 0), sp=(kc == nkc - 1): e.matmul(
                                pv, PT[T][:, 128 * kc:128 * kc + 128],
                                vt[vs_][:, fo:fo + fw], start=st, stop=sp),
                                reads=[b_PT[T][kc // 4], b_vt[vs_]], writes=[bB[T]])
            for T in range(2):
                sg = 4 + T
                bsg = b_tiny[sg]
                rsc, lsum = tiny(sg, 2), tiny(sg, 4)
                p.step("DVE", lambda e, T=T, lsum=lsum, g=g: e.reduce_sum(
                    lsum, lp[T][:, 0:g + 1], axis=AX.X), reads=[b_lp[T]], writes=[bsg])
                p.step("DVE", lambda e, rsc=rsc, lsum=lsum: e.reciprocal(rsc, lsum),
                       reads=[bsg], writes=[bsg])
                p.step("DVE", lambda e, rsc=rsc: e.tensor_scalar(rsc, rsc, 1.0e4, None, op0=ALU.mult),
                       reads=[bsg], writes=[bsg])
            for sweep in range(2):
                if sweep == 1:
                    for kc in range(nkc):
                        vs_ = kc % 2
                        p.dma("SYNC", f"vt{vs_}", lambda e, kc=kc, vs_=vs_: e.dma_start(
                            out=vt[vs_], in_=vS[128 * kc:128 * (kc + 1), EH:E]),
                            reads=[b_vS], writes=[b_vt[vs_]])
                        for T in range(2):
                            for (fo, fw) in _splits(EH):
                                if T == 0 and PVW <= 512:
                                    pv = psA[:, fo:fo + fw]
                                    wb_ = bA[0]
                                elif T == 1 and PVW <= 512:
                                    pv = psA[:, 512 + fo:512 + fo + fw]
                                    wb_ = bA[1]
                                elif T == 0:
                                    pv = psA[:, fo:fo + fw]
                                    wb_ = bA[fo // 512]
                                else:
                                    pv = psN[:, fo:fo + fw]
                                    wb_ = bT[fo // 512]
                                p.step("PE", lambda e, pv=pv, T=T, kc=kc, vs_=vs_, fo=fo, fw=fw,
                                       st=(kc == 0), sp=(kc == nkc - 1): e.matmul(
                                    pv, PT[T][:, 128 * kc:128 * kc + 128],
                                    vt[vs_][:, fo:fo + fw], start=st, stop=sp),
                                    reads=[b_PT[T][kc // 4], b_vt[vs_]], writes=[wb_])
                for T in range(2):
                    sg = 4 + T
                    bsg = b_tiny[sg]
                    rsc = tiny(sg, 2)
                    if sweep == 0:
                        pvh = psB[:, PVW * T:PVW * T + EH]
                        rdb = [bB[T]]
                    elif PVW <= 512:
                        pvh = psA[:, 512 * T:512 * T + EH]
                        rdb = [bA[T]]
                    elif T == 0:
                        pvh = psA[:, 0:EH]
                        rdb = [bA[0], bA[1]]
                    else:
                        pvh = psN[:, 0:EH]
                        rdb = [bT[0], bT[1]]
                    p.step("ACT", lambda e, pvh=pvh, rsc=rsc: e.activation(
                        ci4, pvh, AF.Copy, scale=rsc, bias=AOFF),
                        reads=rdb + [bsg], writes=[b_ci4])
                    p.step("DVE", lambda e, pvh=pvh, rsc=rsc: e.scalar_tensor_tensor(
                        f4, pvh, rsc, ci4, op0=ALU.mult, op1=ALU.subtract),
                        reads=rdb + [bsg, b_ci4], writes=[b_f4])
                    p.step("DVE", lambda e: e.scalar_tensor_tensor(
                        r14, f4, 0.5 - AOFF, ci4, op0=ALU.is_gt, op1=ALU.add),
                        reads=[b_f4, b_ci4], writes=[b_r14])
                    osl = ost[T][:, EH * sweep:EH * (sweep + 1)]
                    p.step("DVE", lambda e, osl=osl: e.scalar_tensor_tensor(
                        osl, r14, 1.0e-4, osl, op0=ALU.mult, op1=ALU.add),
                        reads=[b_r14, b_ost[T]], writes=[b_ost[T]])
            for T in range(2):
                slot = 2 * g + T
                p.dma("ACT", f"out{T}", lambda e, slot=slot, T=T: e.dma_start(
                    out=out_d[128 * slot:128 * (slot + 1), :], in_=ost[T]),
                    reads=[b_ost[T]], writes=[b_out])
        al.pop()
        p.barrier()

        if trunc is not None:
            for q in QUEUES:
                p.steps[q] = [st for st in p.steps[q] if st.seq <= trunc]
        p.finalize()
        ssems = {}
        for nm in p.slots:
            ssems[nm] = es.enter_context(nc.semaphore("z_" + nm))

        @block.sync
        def _(sync):
            p.emit_queue("SYNC", sync, sems, ssems)

        @block.gpsimd
        def _(g):
            p.emit_queue("GPQ", g, sems, ssems)

        @block.tensor
        def _(t):
            p.emit_queue("PE", t, sems, ssems)

        @block.scalar
        def _(sc):
            p.emit_queue("ACT", sc, sems, ssems)

        @block.vector
        def _(v):
            p.emit_queue("DVE", v, sems, ssems)

    return nc


def make_inputs(x, Wk, Wq, Wv, fk, fb, S, E):
    B = x.shape[0]
    wkT = np.ascontiguousarray(Wk.T)
    wqT = np.ascontiguousarray(Wq.T)
    wvT = np.ascontiguousarray(Wv.T)
    fk = np.ascontiguousarray(fk)
    fb = np.ascontiguousarray(fb.reshape(1, E))
    ident = np.eye(128, dtype=np.float32)
    jj = np.arange(512)[None, :]
    pp = np.arange(128)[:, None]
    masks = {}
    for par in (0, 1):
        m = np.empty((2, 128, 512), dtype=np.float32)
        for T in (0, 1):
            v = 2 * par + T
            m[T] = np.where(jj <= pp + 128 * v, 0.0, NEGBIG).astype(np.float32)
        masks[par] = m
    in_maps = []
    for c in range(2 * B):
        b, par = c // 2, c % 2
        xb = np.ascontiguousarray(x[b])
        xown = np.ascontiguousarray(
            xb.reshape(S // 256, 256, E)[par::2].reshape(S // 2, E))
        in_maps.append({
            "emb": xb, "x_own": xown,
            "wkT": wkT, "wqT": wqT, "wvT": wvT,
            "fker": fk, "fbias": fb,
            "masks": masks[par], "ident": ident,
        })
    return in_maps


def assemble_out(results, B, S, E):
    out = np.empty((B, S, E), dtype=np.float32)
    for c in range(2 * B):
        b, par = c // 2, c % 2
        out[b].reshape(S // 256, 256, E)[par::2] = \
            np.asarray(results[c]["out"]).reshape(S // 512, 256, E)
    return out


_NC_CACHE = {}


def kernel(L0_attn_embedded, L0_attn_Wk, L0_attn_Wq, L0_attn_Wv,
           L0_ffn_kernel, L0_ffn_bias):
    S, E = 4096, 2048
    x = np.ascontiguousarray(np.asarray(L0_attn_embedded, dtype=np.float32))
    B = x.shape[0]
    in_maps = make_inputs(
        x,
        np.asarray(L0_attn_Wk, dtype=np.float32),
        np.asarray(L0_attn_Wq, dtype=np.float32),
        np.asarray(L0_attn_Wv, dtype=np.float32),
        np.asarray(L0_ffn_kernel, dtype=np.float32),
        np.asarray(L0_ffn_bias, dtype=np.float32),
        S, E)
    if (S, E) not in _NC_CACHE:
        _NC_CACHE[(S, E)] = build_nc(S, E)
    nc = _NC_CACHE[(S, E)]
    res = run_bass_kernel_spmd(nc, in_maps, core_ids=list(range(2 * B)))
    return assemble_out(res.results, B, S, E)
